# version 20
# speedup vs baseline: 1.6352x; 1.0182x over previous
import numpy as np
import ml_dtypes
import contextlib
import concourse.bass as bass
import concourse.mybir as mybir
from concourse.bass_utils import run_bass_kernel_spmd

F32 = mybir.dt.float32
BF16 = mybir.dt.bfloat16
I32 = mybir.dt.int32
AF = mybir.ActivationFunctionType
ALU = mybir.AluOpType

_DEBUG = False

NCORES = 8
SPC = 4
N = 2048
D = 1024
H = 16
HD = 64
DH = 4096
T = 512
NT = N // T
NSUB = T // 128
NCH = D // 128
NHP = H // 2

TWO_PI = float(np.float32(2 * np.pi))
TWO_PI_LO = float(np.float64(2 * np.pi) - np.float64(np.float32(2 * np.pi)))
INV_2PI = float(1.0 / (2 * np.pi))
PI_2 = float(np.pi / 2)


def _perm_qk():
    p1 = np.zeros(64, np.int64)
    p1[0::2] = np.arange(32)
    p1[1::2] = np.arange(32, 64)
    return np.concatenate([64 * h + p1 for h in range(H)])


def _stage_mats(w, j):
    s = 1 << j
    out = np.zeros((32, 128, 128), np.float32)
    ii = np.arange(128)
    i = ii
    a_sel = (i // s) % 2 == 0
    ia = i[a_sel]
    for c in range(32):
        gi = 128 * c + ia
        k = (gi // (2 * s)) * s + gi % s
        out[c, ia, ia] = w[j, k, 0, 0]
        out[c, ia, ia + s] = w[j, k, 0, 1]
        out[c, ia + s, ia] = w[j, k, 1, 0]
        out[c, ia + s, ia + s] = w[j, k, 1, 1]
    return out


def _big_stage_vecs(w, j, transpose2x2):
    s = 1 << j
    wS = np.zeros((32, 128), np.float32)
    wP = np.zeros((32, 128), np.float32)
    ii = np.arange(128)
    for c in range(32):
        i = 128 * c + ii
        is_a = (i // s) % 2 == 0
        ia = np.where(is_a, i, i - s)
        k = (ia // (2 * s)) * s + ia % s
        wS[c] = np.where(is_a, w[j, k, 0, 0], w[j, k, 1, 1])
        if not transpose2x2:
            wP[c] = np.where(is_a, w[j, k, 0, 1], w[j, k, 1, 0])
        else:
            wP[c] = np.where(is_a, w[j, k, 1, 0], w[j, k, 0, 1])
    return wS, wP


def _host_stage(inputs):
    f4 = np.float32
    Wq = np.asarray(inputs["L0_attn_Wq"], f4)
    Wk = np.asarray(inputs["L0_attn_Wk"], f4)
    upw = np.asarray(inputs["L0_ffn_up_weights"], f4)
    dnw = np.asarray(inputs["L0_ffn_down_weights"], f4)
    perm = _perm_qk()
    d = {}
    d["wq"] = np.ascontiguousarray(Wq[:, perm])
    d["wk"] = np.ascontiguousarray(Wk[:, perm])
    d["wv"] = np.ascontiguousarray(np.asarray(inputs["L0_attn_Wv"], f4))
    d["wo"] = np.ascontiguousarray(np.asarray(inputs["L0_attn_Wo"], f4))
    d["wg1"] = np.ascontiguousarray(np.asarray(inputs["L0_attn_Wg1"], f4))
    d["wg2"] = np.ascontiguousarray(np.asarray(inputs["L0_attn_Wg2"], f4))
    wea = np.zeros((DH, 1032), f4)
    wea[:, :D] = np.asarray(inputs["L0_ffn_W_exp"], f4).T
    wea[:, D] = np.asarray(inputs["L0_ffn_b_exp"], f4)
    d["wea"] = wea
    d["wcon"] = np.ascontiguousarray(np.asarray(inputs["L0_ffn_W_con"], f4))

    cu = np.zeros((7, 32, 128, 128), f4)
    cu[0] = _stage_mats(upw, 6).transpose(0, 2, 1)
    for idx, j in enumerate([5, 4, 3, 2, 1, 0]):
        cu[1 + idx] = _stage_mats(upw, j)
    d["comp_up"] = cu
    cd = np.zeros((7, 32, 128, 128), f4)
    cd[0] = _stage_mats(dnw, 0)
    for idx, j in enumerate([1, 2, 3, 4, 5, 6]):
        cd[1 + idx] = _stage_mats(dnw, j).transpose(0, 2, 1)
    d["comp_dn"] = cd

    wb = np.zeros((2, 5, 32, 2, 128), f4)
    for si, j in enumerate(range(7, 12)):
        wS, wP = _big_stage_vecs(upw, j, False)
        wb[0, si, :, 0, :] = wS
        wb[0, si, :, 1, :] = wP
        wS, wP = _big_stage_vecs(dnw, j, True)
        wb[1, si, :, 0, :] = wS
        wb[1, si, :, 1, :] = wP
    d["wbig"] = np.ascontiguousarray(wb.reshape(640, 128).T)

    p = np.arange(128)
    tf = (p % 64) // 2
    nv = np.zeros((128, 52), f4)
    nv[:, 0] = 10000.0 ** (-2.0 * tf / 64.0)
    nv[:, 1] = np.where(p % 2 == 0, -1.0, 1.0)
    nv[:, 2:10] = np.asarray(inputs["L0_attn_ln_w"], f4).reshape(8, 128).T
    nv[:, 10:18] = np.asarray(inputs["L0_attn_ln_b"], f4).reshape(8, 128).T
    nv[:, 18:50] = np.asarray(inputs["L0_ffn_up_bias"], f4).reshape(32, 128).T
    d["vecs"] = nv
    d["bcon_row"] = np.asarray(inputs["L0_ffn_b_con"], f4).reshape(1, D)
    d["dbias_row"] = np.asarray(inputs["L0_ffn_down_bias"], f4).reshape(1, D)
    return d


def build_nc():
    nc = bass.Bass("TRN2", target_bir_lowering=False)

    ap_x = nc.dram_tensor("x", [SPC, N, D], F32, kind="ExternalInput").ap()
    ap_wq = nc.dram_tensor("wq", [D, D], F32, kind="ExternalInput").ap()
    ap_wk = nc.dram_tensor("wk", [D, D], F32, kind="ExternalInput").ap()
    ap_wv = nc.dram_tensor("wv", [D, D], F32, kind="ExternalInput").ap()
    ap_wo = nc.dram_tensor("wo", [D, D], F32, kind="ExternalInput").ap()
    ap_wg1 = nc.dram_tensor("wg1", [D, HD], F32, kind="ExternalInput").ap()
    ap_wg2 = nc.dram_tensor("wg2", [HD, D], F32, kind="ExternalInput").ap()
    ap_wea = nc.dram_tensor("wea", [DH, 1032], F32, kind="ExternalInput").ap()
    ap_wcon = nc.dram_tensor("wcon", [DH, D], F32, kind="ExternalInput").ap()
    ap_cu = nc.dram_tensor("comp_up", [7, 32, 128, 128], F32, kind="ExternalInput").ap()
    ap_cd = nc.dram_tensor("comp_dn", [7, 32, 128, 128], F32, kind="ExternalInput").ap()
    ap_wbig = nc.dram_tensor("wbig", [128, 640], F32, kind="ExternalInput").ap()
    ap_vecs = nc.dram_tensor("vecs", [128, 52], F32, kind="ExternalInput").ap()
    ap_bcon = nc.dram_tensor("bcon_row", [1, D], F32, kind="ExternalInput").ap()
    ap_dbias = nc.dram_tensor("dbias_row", [1, D], F32, kind="ExternalInput").ap()
    ap_y = nc.dram_tensor("y", [SPC, N, D], F32, kind="ExternalOutput").ap()

    dbg = {}

    def dbg_out(name, shape, dt):
        if name not in dbg:
            dbg[name] = nc.dram_tensor("dbg_" + name, shape, dt, kind="ExternalOutput").ap()
        return dbg[name]

    def dump(buf_ap, name, shape, dt, waits):
        if not _DEBUG:
            return
        o = dbg_out(name, shape, dt)

        @blk.sync
        def _(y):
            for sem, val in waits:
                W(y, sem, val)
            sY.incd(y.dma_start(out=o, in_=buf_ap))

    scr_x1 = nc.dram_tensor("scr_x1", [SPC, N, D], F32, kind="Internal").ap()
    scr_wa = nc.dram_tensor("scr_wa", [DH, 1032], F32, kind="Internal").ap()
    scr_wb = nc.dram_tensor("scr_wb", [DH, 1032], F32, kind="Internal").ap()
    scr_wc = nc.dram_tensor("scr_wc", [DH, 1032], F32, kind="Internal").ap()
    scr_wd = nc.dram_tensor("scr_wd", [DH, 1032], F32, kind="Internal").ap()
    scr_w1 = nc.dram_tensor("scr_w1", [DH, D], BF16, kind="Internal").ap()
    scr_w2 = nc.dram_tensor("scr_w2", [DH, D], BF16, kind="Internal").ap()

    st = contextlib.ExitStack()
    E = st.enter_context

    class Sem:
        def __init__(self, name):
            self.h = E(nc.semaphore(name))
            self.n = 0

        def inc(self, inst, k=1):
            inst.then_inc(self.h, k)
            self.n += k
            return self.n

        def incd(self, inst):
            return self.inc(inst, 16)

    sY = Sem("sY")
    sQ = Sem("sQ")
    sT = Sem("sT")
    sV = Sem("sV")
    sS = Sem("sS")
    sG = Sem("sG")

    _wseen = {}

    def W(eng, sem, val):
        if val <= 0:
            return
        key = (str(eng.engine), id(sem))
        if _wseen.get(key, -1) >= val:
            return
        _wseen[key] = val
        eng.wait_ge(sem.h, val)

    def VS(v, inst):
        sV.inc(inst)
        _wseen[(str(v.engine), id(sV))] = sV.n
        v.wait_ge(sV.h, sV.n)

    vecs = E(nc.sbuf_tensor("vecs_sb", [128, 52], F32))
    ident = E(nc.sbuf_tensor("ident", [128, 128], BF16))
    ones128 = E(nc.sbuf_tensor("ones128", [128, 1], BF16))
    onesrow = E(nc.sbuf_tensor("onesrow", [1, 128], BF16))
    c2_bf = E(nc.sbuf_tensor("c2_bf", [1, D], BF16))
    c1_sb = E(nc.sbuf_tensor("c1_sb", [128, 32], F32))
    ln_sc = E(nc.sbuf_tensor("ln_sc", [128, 64], F32))
    ln_si = E(nc.sbuf_tensor("ln_si", [128, 8], I32))
    bn6 = E(nc.sbuf_tensor("bn6", [128, 12], F32))

    ps_state = E(nc.psum_tensor("ps_state", [128, 512], F32))
    ps_sA = E(nc.psum_tensor("ps_sA", [128, 512], F32))
    ps_sB = E(nc.psum_tensor("ps_sB", [128, 512], F32))
    ps_o = E(nc.psum_tensor("ps_o", [128, 512], F32))
    ps_p0 = E(nc.psum_tensor("ps_p0", [128, 512], F32))
    ps_p1 = E(nc.psum_tensor("ps_p1", [128, 512], F32))
    ps_m0 = E(nc.psum_tensor("ps_m0", [128, 512], F32))
    ps_m1 = E(nc.psum_tensor("ps_m1", [128, 512], F32))

    guard = {}

    def twait(t, bank):
        for sem, val in guard.pop(id(bank), []):
            W(t, sem, val)

    def reads(bank, sem, val):
        guard.setdefault(id(bank), []).append((sem, val))

    blk = E(nc.Block())

    stA = contextlib.ExitStack()
    EA = stA.enter_context
    wq_sb = EA(nc.sbuf_tensor("wq_sb", [128, NCH * D], BF16))
    wk_sb = EA(nc.sbuf_tensor("wk_sb", [128, NCH * D], BF16))
    wv_sb = EA(nc.sbuf_tensor("wv_sb", [128, NCH * D], BF16))
    wo_sb = EA(nc.sbuf_tensor("wo_sb", [128, NCH * D], BF16))
    wg1_sb = EA(nc.sbuf_tensor("wg1_sb", [128, NCH * HD], BF16))
    wg2_sb = EA(nc.sbuf_tensor("wg2_sb", [64, D], BF16))
    wbig = EA(nc.sbuf_tensor("wbig_sb", [128, 640], F32))
    cos_t = EA(nc.sbuf_tensor("cos_t", [128, N], BF16))
    ssin_t = EA(nc.sbuf_tensor("ssin_t", [128, N], BF16))
    tri_bf = EA(nc.sbuf_tensor("tri_bf", [128, 128], BF16))
    lnb_bf = EA(nc.sbuf_tensor("lnb_bf", [128, 8], BF16))
    co_bc = EA(nc.sbuf_tensor("co_bc", [128, D], F32))
    corow_bf = EA(nc.sbuf_tensor("corow_bf", [1, D], BF16))
    nB_bf = EA(nc.sbuf_tensor("nB_bf", [1, D], BF16))
    murow = EA(nc.sbuf_tensor("murow", [1, T], BF16))
    hlrow = EA(nc.sbuf_tensor("hlrow", [1, 4 * T], BF16))

    st0 = contextlib.ExitStack()
    E0 = st0.enter_context
    it_buf = E0(nc.sbuf_tensor("it_buf", [128, N], I32))
    it_buf2 = E0(nc.sbuf_tensor("it_buf2", [128, N], I32))
    tb0 = E0(nc.sbuf_tensor("tb0", [128, N], F32))
    tb1 = E0(nc.sbuf_tensor("tb1", [128, N], F32))
    wof32 = [E0(nc.sbuf_tensor(f"wof32_{i}", [128, D], F32)) for i in range(2)]
    wor_bf = E0(nc.sbuf_tensor("wor_bf", [128, D], BF16))
    rowa = E0(nc.sbuf_tensor("rowa", [1, D], F32))
    rowb = E0(nc.sbuf_tensor("rowb", [1, D], F32))
    c2row = E0(nc.sbuf_tensor("c2row", [1, D], F32))

    v_wq = v_wk = v_wv = v_wg = 0

    @blk.sync
    def _(y):
        sY.incd(y.dma_start(out=vecs[:], in_=ap_vecs))
        sY.incd(y.dma_start(out=wbig[:], in_=ap_wbig))
        sY.incd(y.dma_start(out=rowa[:], in_=ap_bcon))
        sY.incd(y.dma_start(out=rowb[:], in_=ap_dbias))
    v_rows = sY.n

    @blk.gpsimd
    def _(g):
        nonlocal_marker = None
        sQ.incd(g.dma_start(out=wq_sb[:].rearrange("p (c j) -> p c j", c=NCH), in_=ap_wq.rearrange("(c p) j -> p c j", p=128)))
        sQ.incd(g.dma_start(out=wk_sb[:].rearrange("p (c j) -> p c j", c=NCH), in_=ap_wk.rearrange("(c p) j -> p c j", p=128)))
        sQ.incd(g.dma_start(out=wv_sb[:].rearrange("p (c j) -> p c j", c=NCH), in_=ap_wv.rearrange("(c p) j -> p c j", p=128)))
        sQ.incd(g.dma_start(out=wg1_sb[:].rearrange("p (c j) -> p c j", c=NCH), in_=ap_wg1.rearrange("(c p) j -> p c j", p=128)))
        sQ.incd(g.dma_start(out=wg2_sb[:], in_=ap_wg2))
    v_wload = sQ.n

    @blk.gpsimd
    def _(g):
        sG.inc(g.memset(ones128[:], 1.0))
        sG.inc(g.memset(onesrow[:], 1.0))
        sG.inc(g.iota(it_buf[:, 0:128], [[1, 128]], base=0, channel_multiplier=-1))
    v_iota1 = sG.n

    @blk.vector
    def _(v):
        W(v, sG, v_iota1)
        sV.inc(v.tensor_scalar(ident[:], it_buf[:, 0:128], 0, None, ALU.is_equal))
        sV.inc(v.tensor_scalar(tri_bf[:], it_buf[:, 0:128], 0, None, ALU.is_ge))
        W(v, sY, v_rows)
        VS(v, v.tensor_tensor(c2row[:], rowa[:], rowb[:], ALU.add))
        VS(v, v.tensor_copy(c2_bf[:], c2row[:]))
        sV.inc(v.tensor_copy(lnb_bf[:], vecs[:, 10:18]))
    v_masks = sV.n

    @blk.gpsimd
    def _(g):
        W(g, sV, v_masks)
        sG.inc(g.iota(it_buf[:], [[1, N]], base=0, channel_multiplier=0))
    v_iota2 = sG.n

    def emit_table(shift, out_tile, signed, prev_ss):
        @blk.vector
        def _(v):
            W(v, sG, v_iota2)
            W(v, sY, v_rows)
            W(v, sS, prev_ss)
            VS(v, v.tensor_scalar(tb0[:], it_buf[:], vecs[:, 0:1], None, ALU.mult))
            if shift:
                VS(v, v.tensor_scalar(tb0[:], tb0[:], shift, None, ALU.add))
            VS(v, v.tensor_scalar(tb1[:], tb0[:], INV_2PI, None, ALU.mult))
            VS(v, v.tensor_copy(it_buf2[:], tb1[:]))
            VS(v, v.tensor_copy(tb1[:], it_buf2[:]))
            VS(v, v.scalar_tensor_tensor(tb0[:], tb1[:], -TWO_PI, tb0[:], ALU.mult, ALU.add))
            VS(v, v.scalar_tensor_tensor(tb0[:], tb1[:], -TWO_PI_LO, tb0[:], ALU.mult, ALU.add))
        vv = sV.n

        @blk.scalar
        def _(s):
            W(s, sV, vv)
            if signed:
                sS.inc(s.activation(tb1[:], tb0[:], AF.Sin))
            else:
                sS.inc(s.activation(out_tile[:], tb0[:], AF.Sin))
        ss = sS.n
        if signed:
            @blk.vector
            def _(v):
                W(v, sS, ss)
                sV.inc(v.tensor_scalar(out_tile[:], tb1[:], vecs[:, 1:2], None, ALU.mult))

    emit_table(PI_2, cos_t, False, 0)
    emit_table(0.0, ssin_t, True, sS.n)
    v_tab_s, v_tab_v = sS.n, sV.n

    wo_guard = {}
    co_t = 0
    for c in range(NCH):
        @blk.sync
        def _(y, c=c):
            if c >= 2:
                W(y, sS, wo_guard[c - 2])
            sY.incd(y.dma_start(out=wof32[c % 2][:], in_=ap_wo[128 * c:128 * c + 128, :]))
        v_ld = sY.n

        @blk.scalar
        def _(s, c=c, co_t=co_t):
            W(s, sY, v_ld)
            W(s, sV, v_masks)
            W(s, sT, co_t)
            sS.inc(s.activation(wor_bf[:], wof32[c % 2][:], AF.Identity))
            sS.inc(s.activation(wo_sb[:, D * c:D * (c + 1)], wof32[c % 2][:], AF.Identity,
                                scale=vecs[:, 2 + c:3 + c]))
        wo_guard[c] = sS.n
        v_wor = sS.n

        @blk.tensor
        def _(t, c=c):
            W(t, sS, v_wor)
            for o in range(2):
                bank = ps_m0 if o == 0 else ps_m1
                if c == 0:
                    twait(t, bank)
                sT.inc(t.matmul(bank[0:1, :], lnb_bf[:, c:c + 1], wor_bf[:, 512 * o:512 * (o + 1)],
                                start=(c == 0), stop=(c == NCH - 1)))
        co_t = sT.n
    v_comm = sT.n
    v_wo_ready = wo_guard[NCH - 1]

    @blk.vector
    def _(v):
        W(v, sT, v_comm)
        sV.inc(v.tensor_copy(corow_bf[0:1, 0:512], ps_m0[0:1, :]))
        sV.inc(v.tensor_copy(corow_bf[0:1, 512:1024], ps_m1[0:1, :]))
    v_corow = sV.n
    reads(ps_m0, sV, v_corow)
    reads(ps_m1, sV, v_corow)

    @blk.tensor
    def _(t):
        W(t, sS, wo_guard[NCH - 1])
        for o in range(2):
            bank = ps_p0 if o == 0 else ps_p1
            twait(t, bank)
            for c in range(NCH):
                sT.inc(t.matmul(bank[0:1, :], ones128[:], wo_sb[:, D * c + 512 * o:D * c + 512 * o + 512],
                                start=(c == 0), stop=(c == NCH - 1)))
        W(t, sV, v_corow)
        for o in range(2):
            bank = ps_m0 if o == 0 else ps_m1
            twait(t, bank)
            sT.inc(t.matmul(bank[:], onesrow[:], corow_bf[0:1, 512 * o:512 * (o + 1)],
                            start=True, stop=True))
    v_nbmm = sT.n

    @blk.vector
    def _(v):
        W(v, sT, v_nbmm)
        sV.inc(v.tensor_scalar(nB_bf[0:1, 0:512], ps_p0[0:1, :], -1.0, None, ALU.mult))
        sV.inc(v.tensor_scalar(nB_bf[0:1, 512:1024], ps_p1[0:1, :], -1.0, None, ALU.mult))
        sV.inc(v.tensor_copy(co_bc[:, 0:512], ps_m0[:]))
        sV.inc(v.tensor_copy(co_bc[:, 512:1024], ps_m1[:]))
    v_woprep = sV.n
    for b in (ps_p0, ps_p1, ps_m0, ps_m1):
        reads(b, sV, v_woprep)

    if _DEBUG:
        dump(co_bc[:], "co_bc", [128, D], F32, [(sV, v_woprep)])
        dump(nB_bf[:], "nB_bf", [1, D], BF16, [(sV, v_woprep)])
        dump(wo_sb[:], "wo_sb", [128, NCH * D], BF16, [(sS, v_wo_ready)])

    st0.close()
    xld = [EA(nc.sbuf_tensor(f"xld{i}", [128, D], F32)) for i in range(4)]
    xn_tm = [EA(nc.sbuf_tensor(f"xn_tm{i}", [128, D], BF16)) for i in range(2)]
    xn_fm = EA(nc.sbuf_tensor("xn_fm", [128, NCH * T], BF16))
    z0 = xn_fm
    q_fm = EA(nc.sbuf_tensor("q_fm", [128, NCH * T], BF16))
    k_fm = EA(nc.sbuf_tensor("k_fm", [128, NCH * T], BF16))
    v_tm = EA(nc.sbuf_tensor("v_tm", [128, NSUB * D], BF16))
    k_tm = EA(nc.sbuf_tensor("k_tm", [128, NSUB * D], BF16))
    qsil = [EA(nc.sbuf_tensor(f"qsil{i}", [128, T], BF16)) for i in range(2)]
    qshuf = [EA(nc.sbuf_tensor(f"qshuf{i}", [128, T], BF16)) for i in range(2)]
    qt1 = [EA(nc.sbuf_tensor(f"qt1_{i}", [128, T], BF16)) for i in range(2)]
    s_bf = [[EA(nc.sbuf_tensor(f"s_bf{p}_{i}", [128, T], BF16)) for i in range(2)]
            for p in range(2)]
    st_bf = EA(nc.sbuf_tensor("st_bf", [128, T], BF16))
    g1_bf = EA(nc.sbuf_tensor("g1_bf", [64, T], BF16))
    gth = [EA(nc.sbuf_tensor(f"gth{i}", [128, T], BF16)) for i in range(2)]
    gate_bf = EA(nc.sbuf_tensor("gate_bf", [128, NCH * T], BF16))
    sq_bf = [EA(nc.sbuf_tensor(f"sq_bf{i}", [128, T], BF16)) for i in range(2)]
    x1buf = [EA(nc.sbuf_tensor(f"x1buf{i}", [128, D], F32)) for i in range(2)]
    SPW = 258
    spn = [EA(nc.sbuf_tensor(f"spn{i}", [128, 4 * SPW], F32)) for i in range(2)]
    spt = EA(nc.sbuf_tensor("spt", [128, SPW], F32))
    NLANE = 2
    cmp_in = [EA(nc.sbuf_tensor(f"cmp_in{l}", [128, 7 * 128], BF16)) for l in range(NLANE)]
    cmp_ab = [[EA(nc.sbuf_tensor(f"cmp_ab{l}_{i}", [128, 128], BF16)) for i in range(2)]
              for l in range(NLANE)]
    app_in = [EA(nc.sbuf_tensor(f"app_in{l}", [128, 1032], BF16)) for l in range(NLANE)]
    app_out = EA(nc.sbuf_tensor("app_out", [128, 1032], BF16))

    def prep_units():
        spans = []
        chunks = []
        prereq = {}
        for net, src0 in ((0, ap_wea), (1, ap_wcon)):
            wsrc = 1032 if net == 0 else D
            if net == 0:
                passes = [
                    (src0, scr_wa, [(c, c + 1, c + 2, c + 3) for c in range(0, 32, 4)], [0, 1]),
                    (scr_wa, scr_wb, [(c, c + 4, c + 8, c + 12)
                                      for base in (0, 16) for c in range(base, base + 4)], [2, 3]),
                    (scr_wb, scr_wa, [(c, c + 16) for c in range(16)], [4]),
                ]
            else:
                passes = [
                    (src0, scr_wc, [(c, c + 16) for c in range(16)], [4]),
                    (scr_wc, scr_wd, [(c, c + 4, c + 8, c + 12)
                                      for base in (0, 16) for c in range(base, base + 4)], [3, 2]),
                    (scr_wd, scr_wc, [(c, c + 1, c + 2, c + 3) for c in range(0, 32, 4)], [1, 0]),
                ]
            for src, dst, groups, stages in passes:
                for grp in groups:
                    for h in range(4):
                        col0 = SPW * h
                        cw = min(SPW, wsrc - col0)
                        if cw > 0:
                            spans.append(("span", net, src, dst, grp, stages, col0, cw))
            prereq[net] = len(spans)
            for c in range(32):
                chunks.append(("chunk", net, wsrc, c, prereq[net]))
        return spans, chunks

    span_state = {"prev": [], "n": 0}

    def emit_span(net, src, dst, grp, stages, col0, cw):
        nchk = len(grp)
        A, B = spn[0], spn[1]

        @blk.gpsimd
        def _(g):
            for s_, v_ in span_state["prev"]:
                W(g, s_, v_)
            for i, c in enumerate(grp):
                sQ.incd(g.dma_start(out=A[:, SPW * i:SPW * i + cw],
                                    in_=src[128 * c:128 * c + 128, col0:col0 + cw]))
        v_ld = sQ.n
        cur, nxt = A, B
        use_g = False
        span_state["n"] += 1
        dsem = sG if use_g else sV

        def emit_ops(e):
            def SW(inst):
                dsem.inc(inst)
                _wseen[(str(e.engine), id(dsem))] = dsem.n
                e.wait_ge(dsem.h, dsem.n)
            cur2, nxt2 = cur, nxt
            for stj in stages:
                for oi in range(nchk):
                    c_out = grp[oi]
                    dist = 1 << stj
                    pi = grp.index(c_out ^ dist)
                    wcol_s = ((net * 5 + stj) * 32 + c_out) * 2
                    wcol_p = wcol_s + 1
                    W(e, sQ, v_ld)
                    SW(e.tensor_scalar(spt[:, 0:cw], cur2[:, SPW * pi:SPW * pi + cw],
                                       wbig[:, wcol_p:wcol_p + 1], None, ALU.mult))
                    SW(e.scalar_tensor_tensor(nxt2[:, SPW * oi:SPW * oi + cw],
                                              cur2[:, SPW * oi:SPW * oi + cw],
                                              wbig[:, wcol_s:wcol_s + 1],
                                              spt[:, 0:cw], ALU.mult, ALU.add))
                cur2, nxt2 = nxt2, cur2
            return cur2

        fin = [None]
        if use_g:
            @blk.gpsimd
            def _(g):
                fin[0] = emit_ops(g)
        else:
            @blk.vector
            def _(v):
                fin[0] = emit_ops(v)
        v_done = (dsem, dsem.n)
        cur = fin[0]

        @blk.gpsimd
        def _(g):
            W(g, v_done[0], v_done[1])
            for i, c in enumerate(grp):
                sQ.incd(g.dma_start(out=dst[128 * c:128 * c + 128, col0:col0 + cw],
                                    in_=cur[:, SPW * i:SPW * i + cw]))
        span_state["prev"] = [(sQ, sQ.n), v_done]

    lane_banks = [(ps_m0, ps_m1), (ps_p0, ps_p1)]
    chunk_prev = [{"q": 0, "v": 0} for _ in range(NLANE)]

    def emit_chunk_loads(units):
        n = len(units)
        lds = [0] * n
        for li, (net, wsrc, c) in enumerate(units):
            @blk.gpsimd
            def _(g, li=li, net=net, wsrc=wsrc, c=c):
                capp = ap_cu if net == 0 else ap_cd
                srcw = scr_wa if net == 0 else scr_wc
                if chunk_prev[li]["v"]:
                    W(g, sV, chunk_prev[li]["v"])
                for j in range(7):
                    sQ.incd(g.dma_start(out=cmp_in[li][:, 128 * j:128 * (j + 1)], in_=capp[j, c]))
                sQ.incd(g.dma_start(out=app_in[li][:, 0:wsrc],
                                    in_=srcw[128 * c:128 * c + 128, 0:wsrc]))
            lds[li] = sQ.n
        return lds

    def emit_chunk_compute(units, lds):
        n = len(units)
        vvs = [0] * n
        curs = [cmp_in[li][:, 0:128] for li in range(n)]
        for step in range(6):
            for li in range(n):
                bank0 = lane_banks[li][0]

                @blk.tensor
                def _(t, li=li, step=step, bank0=bank0, cur=curs[li], vv=vvs[li]):
                    W(t, sQ, lds[li])
                    if vv:
                        W(t, sV, vv)
                    twait(t, bank0)
                    sT.inc(t.matmul(bank0[:, 0:128],
                                    cmp_in[li][:, 128 * (step + 1):128 * (step + 2)],
                                    cur, start=True, stop=True))
                tt = sT.n

                @blk.vector
                def _(v, li=li, step=step, tt=tt, bank0=bank0):
                    W(v, sT, tt)
                    sV.inc(v.tensor_copy(cmp_ab[li][step % 2][:], bank0[:, 0:128]))
                vvs[li] = sV.n
                reads(bank0, sV, vvs[li])
                curs[li] = cmp_ab[li][step % 2][:]
        aps = [0] * n
        for li in range(n):
            bank0, bank1 = lane_banks[li]

            @blk.tensor
            def _(t, li=li, bank0=bank0, bank1=bank1):
                W(t, sV, vvs[li])
                twait(t, bank0)
                twait(t, bank1)
                sT.inc(t.matmul(bank0[:], curs[li], app_in[li][:, 0:512], start=True, stop=True))
                sT.inc(t.matmul(bank1[:], curs[li], app_in[li][:, 512:1024], start=True, stop=True))
            aps[li] = sT.n
        for li, (net, wsrc, c) in enumerate(units):
            bank0, bank1 = lane_banks[li]

            @blk.vector
            def _(v, li=li, bank0=bank0, bank1=bank1):
                W(v, sT, aps[li])
                bwait(v, app_out)
                sV.inc(v.tensor_copy(app_out[:, 0:512], bank0[:]))
                sV.inc(v.tensor_copy(app_out[:, 512:1024], bank1[:]))
            ev = sV.n
            reads(bank0, sV, ev)
            reads(bank1, sV, ev)
            if net == 0:
                @blk.tensor
                def _(t, li=li, ev=ev, bank1=bank1):
                    W(t, sV, ev)
                    twait(t, bank1)
                    sT.inc(t.matmul(bank1[:, 0:8], curs[li], app_in[li][:, 1024:1032],
                                    start=True, stop=True))
                t_b = sT.n

                @blk.vector
                def _(v, c=c, t_b=t_b, bank1=bank1):
                    W(v, sT, t_b)
                    sV.inc(v.tensor_copy(c1_sb[:, c:c + 1], bank1[:, 0:1]))
                reads(bank1, sV, sV.n)
                ev = sV.n
            dstw = scr_w1 if net == 0 else scr_w2

            @blk.gpsimd
            def _(g, ev=ev, c=c, dstw=dstw):
                W(g, sV, ev)
                sQ.incd(g.dma_start(out=dstw[128 * c:128 * c + 128, :], in_=app_out[:, 0:1024]))
            bread(app_out, sQ, sQ.n)
            chunk_prev[li]["v"] = ev
            chunk_prev[li]["q"] = sQ.n

    PREP_SPANS, PREP_CHUNKS = prep_units()
    span_i = [0]
    chunk_i = [0]
    prep_pend = [None]

    def emit_prep_spans(k):
        while k > 0 and span_i[0] < len(PREP_SPANS):
            u = PREP_SPANS[span_i[0]]
            span_i[0] += 1
            k -= 1
            emit_span(u[1], u[2], u[3], u[4], u[5], u[6], u[7])

    def emit_prep_chunks(k):
        while k > 0:
            if prep_pend[0] is not None:
                units, lds = prep_pend[0]
                prep_pend[0] = None
                emit_chunk_compute(units, lds)
                k -= len(units)
                continue
            grp = []
            while (len(grp) < NLANE and chunk_i[0] < len(PREP_CHUNKS)
                   and PREP_CHUNKS[chunk_i[0]][4] <= span_i[0]):
                grp.append(PREP_CHUNKS[chunk_i[0]][1:4])
                chunk_i[0] += 1
            if not grp:
                return
            prep_pend[0] = (grp, emit_chunk_loads(grp))
            if k <= len(grp):
                return

    def emit_prep_slice(k):
        emit_prep_spans(k)
        emit_prep_chunks(k)

    bufg = {}

    def bwait(eng, buf):
        for sem, val in bufg.pop(id(buf), []):
            W(eng, sem, val)

    def bread(buf, sem, val):
        bufg.setdefault(id(buf), []).append((sem, val))

    def bpop(buf):
        return bufg.pop(id(buf), [])

    class SBatch:

        def __init__(self):
            self.q = []
            self.virt = 0

        def add(self, waits, fn):
            self.q.append((list(waits), fn))
            self.virt += 1
            return sS.n + self.virt

        def flush(self):
            if not self.q:
                return
            q, promised = self.q, sS.n + self.virt
            self.q, self.virt = [], 0

            @blk.scalar
            def _(sc):
                for waits, fn in q:
                    for sem, val in waits:
                        W(sc, sem, val)
                    sS.inc(fn(sc))
            assert sS.n == promised, (sS.n, promised)

    def emit_x_stats(s2, t02, sub):
        @blk.sync
        def _(y, sub=sub):
            bwait(y, xld[sub])
            sY.incd(y.dma_start(out=xld[sub][:],
                                in_=ap_x[s2, t02 + 128 * sub:t02 + 128 * (sub + 1), :]))
        xv = sY.n

        @blk.vector
        def _(v, sub=sub, xv=xv):
            W(v, sY, xv)
            sV.inc(v.bn_stats(bn6[:, 0:6], xld[sub][:, 0:512]))
            VS(v, v.bn_stats(bn6[:, 6:12], xld[sub][:, 512:1024]))
            VS(v, v.bn_aggr(ln_sc[:, 2 * sub:2 * sub + 2], bn6[:]))

    def emit_ln1_newton():
        @blk.vector
        def _(v):
            mean = ln_sc[:, 0:8].rearrange("p (s two) -> p s two", two=2)[:, :, 0]
            var = ln_sc[:, 0:8].rearrange("p (s two) -> p s two", two=2)[:, :, 1]
            VS(v, v.tensor_scalar(ln_sc[:, 8:12], var, 1e-5, None, ALU.add))
            VS(v, v.tensor_scalar(ln_si[:, 0:4], ln_sc[:, 8:12].bitcast(I32), 1, None,
                                  ALU.logical_shift_right))
            VS(v, v.tensor_scalar(ln_si[:, 0:4], ln_si[:, 0:4], -1, 0x5f3759df,
                                  ALU.mult, ALU.add))
            VS(v, v.tensor_copy(ln_sc[:, 12:16], ln_si[:, 0:4].bitcast(F32)))
            for _ in range(3):
                VS(v, v.tensor_tensor(ln_sc[:, 16:20], ln_sc[:, 12:16], ln_sc[:, 12:16], ALU.mult))
                VS(v, v.tensor_tensor(ln_sc[:, 16:20], ln_sc[:, 16:20], ln_sc[:, 8:12], ALU.mult))
                VS(v, v.tensor_scalar(ln_sc[:, 16:20], ln_sc[:, 16:20], -0.5, 1.5,
                                      ALU.mult, ALU.add))
                VS(v, v.tensor_tensor(ln_sc[:, 12:16], ln_sc[:, 12:16], ln_sc[:, 16:20], ALU.mult))
            VS(v, v.tensor_tensor(ln_sc[:, 20:24], mean, ln_sc[:, 12:16], ALU.mult))
            VS(v, v.tensor_scalar(ln_sc[:, 20:24], ln_sc[:, 20:24], -1.0, None, ALU.mult))
            VS(v, v.tensor_tensor(ln_sc[:, 52:56], ln_sc[:, 12:16], ln_sc[:, 8:12], ALU.mult))

    st_ev = None
    for sub in range(NSUB):
        emit_x_stats(0, 0, sub)
    emit_ln1_newton()

    for s in range(SPC):
        for tau in range(NT):
            t0 = T * tau
            tile = s * NT + tau

            sn = []
            for sub in range(NSUB):
                xnb = xn_tm[sub % 2]

                @blk.vector
                def _(v, sub=sub, xnb=xnb):
                    bwait(v, xnb)
                    VS(v, v.tensor_scalar(xld[sub][:], xld[sub][:],
                                          ln_sc[:, 12 + sub:13 + sub], None, ALU.mult))
                    VS(v, v.tensor_scalar(xnb[:], xld[sub][:],
                                          ln_sc[:, 20 + sub:21 + sub], None, ALU.add))
                sn.append(sV.n)
                if tile == 0 and sub == 0:
                    dump(xnb[:], "xn_tm0", [128, D], BF16, [(sV, sV.n)])

                bank = ps_m0 if sub % 2 == 0 else ps_m1

                @blk.tensor
                def _(t, sub=sub, bank=bank, xnb=xnb):
                    W(t, sV, sn[sub])
                    twait(t, bank)
                    pb = bank[:].bitcast(BF16)
                    for cc in range(NCH):
                        sT.inc(t.transpose(pb[:, 128 * cc:128 * (cc + 1)],
                                           xnb[:, 128 * cc:128 * (cc + 1)], ident[:]))
                t_tp = sT.n
                bread(xnb, sT, sT.n)

                @blk.vector
                def _(v, sub=sub, bank=bank, t_tp=t_tp):
                    W(v, sT, t_tp)
                    out = xn_fm[:].rearrange("p (c t) -> p c t", c=NCH)[:, :, 128 * sub:128 * (sub + 1)]
                    sV.inc(v.tensor_copy(out, bank[:].bitcast(BF16).rearrange("p (c t) -> p c t", c=NCH)))
                reads(bank, sV, sV.n)
            v_xnfm = sV.n
            if tile == 0:
                dump(xn_fm[:], "xn_fm", [128, NCH * T], BF16, [(sV, v_xnfm)])

            vv_done = {}
            def proj_rope(dst_fm, w_sb, wload_v):
                sb = SBatch()
                mdone = []
                for m in range(NCH):
                    bank = ps_p0 if m % 2 == 0 else ps_p1
                    qs, qh, t1 = qsil[m % 2], qshuf[m % 2], qt1[m % 2]

                    @blk.tensor
                    def _(t, m=m, bank=bank):
                        W(t, sV, v_xnfm)
                        if tile == 0:
                            W(t, sQ, wload_v)
                        twait(t, bank)
                        for cc in range(NCH):
                            sT.inc(t.matmul(bank[:], w_sb[:, D * cc + 128 * m:D * cc + 128 * m + 128],
                                            xn_fm[:, T * cc:T * (cc + 1)],
                                            start=(cc == 0), stop=(cc == NCH - 1)))
                    t_mm = sT.n

                    s_si = sb.add([(sT, t_mm)] + bpop(qs),
                                  lambda sc, bank=bank, qs=qs: sc.activation(qs[:], bank[:], AF.Silu))
                    reads(bank, sS, s_si)

                    @blk.vector
                    def _(v, m=m, s_si=s_si, qs=qs, qh=qh):
                        W(v, sS, s_si)
                        bwait(v, qh)
                        mask = []
                        for i in range(0, 32, 2):
                            mask += [i + 1, i]
                        sV.inc(v.stream_shuffle(qh[:], qs[:], mask))
                    v_sh = sV.n

                    @blk.gpsimd
                    def _(g, m=m, s_si=s_si, v_sh=v_sh, qs=qs, qh=qh, t1=t1):
                        W(g, sS, s_si)
                        W(g, sV, v_sh)
                        if tile == 0:
                            W(g, sS, v_tab_s)
                            W(g, sV, v_tab_v)
                        bwait(g, t1)
                        sG.inc(g.tensor_tensor(t1[:], qs[:],
                                               cos_t[:, t0:t0 + T], ALU.mult))
                        sG.inc(g.tensor_tensor(qh[:], qh[:],
                                               ssin_t[:, t0:t0 + T], ALU.mult))
                    g_mu = sG.n
                    bread(qs, sG, g_mu)

                    @blk.vector
                    def _(v, m=m, g_mu=g_mu, t1=t1, qh=qh):
                        W(v, sG, g_mu)
                        sV.inc(v.tensor_tensor(dst_fm[:, T * m:T * (m + 1)], t1[:],
                                               qh[:], ALU.add))
                    bread(t1, sV, sV.n)
                    bread(qh, sV, sV.n)
                    mdone.append(sV.n)
                sb.flush()
                return mdone

            q_done = proj_rope(q_fm, wq_sb, v_wload)
            v_q = q_done[-1]
            if tile == 0:
                dump(q_fm[:], "q_fm", [128, NCH * T], BF16, [(sV, v_q)])
            k_done = proj_rope(k_fm, wk_sb, v_wload)
            v_k = k_done[-1]
            if tile == 0:
                dump(k_fm[:], "k_fm", [128, NCH * T], BF16, [(sV, v_k)])

            for sub in range(NSUB):
                for o in range(2):
                    bank = ps_p0 if (2 * sub + o) % 2 == 0 else ps_p1

                    @blk.tensor
                    def _(t, sub=sub, o=o, bank=bank):
                        W(t, sV, v_xnfm)
                        if tile == 0:
                            W(t, sQ, v_wload)
                        twait(t, bank)
                        for cc in range(NCH):
                            sT.inc(t.matmul(bank[:],
                                            xn_fm[:, T * cc + 128 * sub:T * cc + 128 * sub + 128],
                                            wv_sb[:, D * cc + 512 * o:D * cc + 512 * o + 512],
                                            start=(cc == 0), stop=(cc == NCH - 1)))
                    t_mm = sT.n

                    @blk.vector
                    def _(v, sub=sub, o=o, bank=bank, t_mm=t_mm):
                        W(v, sT, t_mm)
                        sV.inc(v.tensor_copy(v_tm[:, D * sub + 512 * o:D * sub + 512 * o + 512], bank[:]))
                    reads(bank, sV, sV.n)
                    vv_done[(sub, o)] = sV.n
            v_v = sV.n
            if tile == 0:
                dump(v_tm[:], "v_tm", [128, NSUB * D], BF16, [(sV, v_v)])

            @blk.tensor
            def _(t):
                W(t, sV, v_xnfm)
                if tile == 0:
                    W(t, sQ, v_wload)
                twait(t, ps_p0)
                for cc in range(NCH):
                    sT.inc(t.matmul(ps_p0[0:64, :], wg1_sb[:, 64 * cc:64 * (cc + 1)],
                                    xn_fm[:, T * cc:T * (cc + 1)],
                                    start=(cc == 0), stop=(cc == NCH - 1)))
            t_g1 = sT.n

            @blk.vector
            def _(v):
                W(v, sT, t_g1)
                sV.inc(v.tensor_copy(g1_bf[:], ps_p0[0:64, :]))
            v_g1 = sV.n
            reads(ps_p0, sV, v_g1)

            sbg = SBatch()
            for m in range(NCH):
                bank = ps_m0 if m % 2 == 0 else ps_m1
                gt = gth[m % 2]

                @blk.tensor
                def _(t, m=m, bank=bank):
                    W(t, sV, v_g1)
                    twait(t, bank)
                    sT.inc(t.matmul(bank[:], wg2_sb[0:64, 128 * m:128 * (m + 1)], g1_bf[:],
                                    start=True, stop=True))
                t_G = sT.n

                s_th = sbg.add([(sT, t_G)] + bpop(gt),
                               lambda sc, bank=bank, gt=gt: sc.activation(gt[:], bank[:],
                                                                          AF.Tanh, scale=0.5))
                reads(bank, sS, s_th)

                @blk.gpsimd
                def _(g, m=m, s_th=s_th, gt=gt):
                    W(g, sS, s_th)
                    if m == 0:
                        bwait(g, gate_bf)
                    sG.inc(g.tensor_scalar(gate_bf[:, T * m:T * (m + 1)], gt[:],
                                           0.5, 0.5, ALU.mult, ALU.add))
                bread(gt, sG, sG.n)
            sbg.flush()
            v_gate = sG.n
            if tile == 0:
                dump(gate_bf[:], "gate_bf", [128, NCH * T], BF16, [(sG, v_gate)])

            kt_done = {}
            for sub in range(NSUB):
                bank = ps_m0 if sub % 2 == 0 else ps_m1

                @blk.tensor
                def _(t, sub=sub, bank=bank):
                    W(t, sV, v_k)
                    W(t, sG, v_gate)
                    twait(t, bank)
                    pb = bank[:].bitcast(BF16)
                    for hp in range(NHP):
                        sT.inc(t.transpose(pb[:, 128 * hp:128 * (hp + 1)],
                                           k_fm[:, T * hp + 128 * sub:T * hp + 128 * (sub + 1)],
                                           ident[:]))
                t_tp = sT.n

                @blk.vector
                def _(v, sub=sub, bank=bank, t_tp=t_tp):
                    W(v, sT, t_tp)
                    sV.inc(v.tensor_copy(k_tm[:, D * sub:D * (sub + 1)], bank[:].bitcast(BF16)))
                reads(bank, sV, sV.n)
                kt_done[sub] = sV.n
            v_ktm = sV.n
            if tile == 0:
                dump(k_tm[:], "k_tm", [128, NSUB * D], BF16, [(sV, v_ktm)])

            z0_done = {}
            for hp in range(NHP):
                hA, hB = 2 * hp, 2 * hp + 1

                @blk.tensor
                def _(t, hp=hp):
                    twait(t, ps_o)
                    W(t, sV, q_done[hp])
                    if tau > 0:
                        W(t, sV, st_ev)
                        sT.inc(t.matmul(ps_o[0:64, :], st_bf[0:64, 64 * hp:64 * hp + 64],
                                        q_fm[0:64, T * hp:T * (hp + 1)],
                                        start=True, stop=False, tile_position=(0, 0)))
                        sT.inc(t.matmul(ps_o[64:128, :], st_bf[64:128, 64 * hp:64 * hp + 64],
                                        q_fm[64:128, T * hp:T * (hp + 1)],
                                        start=True, stop=False, tile_position=(64, 64)))
                        bread(st_bf, sT, sT.n)

                tri_v = {}

                def emit_score(ksub):
                    q0 = 128 * ksub
                    sbp = s_bf[ksub % 2]
                    bA, bB = (ps_sA, ps_sB) if ksub % 2 == 0 else (ps_p0, ps_p1)

                    @blk.tensor
                    def _(t, hp=hp, ksub=ksub, q0=q0, bA=bA, bB=bB):
                        W(t, sV, q_done[hp])
                        W(t, sV, k_done[hp])
                        twait(t, bA)
                        twait(t, bB)
                        sT.inc(t.matmul(bA[:, q0:T],
                                        k_fm[0:64, T * hp + q0:T * hp + q0 + 128],
                                        q_fm[0:64, T * hp + q0:T * (hp + 1)],
                                        start=True, stop=True, tile_position=(0, 0)))
                        sT.inc(t.matmul(bB[:, q0:T],
                                        k_fm[64:128, T * hp + q0:T * hp + q0 + 128],
                                        q_fm[64:128, T * hp + q0:T * (hp + 1)],
                                        start=True, stop=True, tile_position=(64, 0)))
                    t_s = sT.n

                    @blk.vector
                    def _(v, ksub=ksub, q0=q0, sbp=sbp, bA=bA, bB=bB, t_s=t_s):
                        W(v, sT, t_s)
                        bwait(v, sbp[0])
                        bwait(v, sbp[1])
                        sV.inc(v.tensor_tensor(sbp[0][:, q0:q0 + 128], bA[:, q0:q0 + 128],
                                               tri_bf[:], ALU.mult))
                        sV.inc(v.tensor_tensor(sbp[1][:, q0:q0 + 128], bB[:, q0:q0 + 128],
                                               tri_bf[:], ALU.mult))
                        if ksub < NSUB - 1:
                            sV.inc(v.tensor_copy(sbp[0][:, q0 + 128:T], bA[:, q0 + 128:T]))
                            sV.inc(v.tensor_copy(sbp[1][:, q0 + 128:T], bB[:, q0 + 128:T]))
                    tri_v[ksub] = sV.n
                    reads(bA, sV, tri_v[ksub])
                    reads(bB, sV, tri_v[ksub])

                def emit_ev(ksub):
                    q0 = 128 * ksub
                    sbp = s_bf[ksub % 2]

                    @blk.tensor
                    def _(t, hp=hp, ksub=ksub, q0=q0, hA=hA, hB=hB, sbp=sbp):
                        W(t, sV, tri_v[ksub])
                        W(t, sV, vv_done[(ksub, hp // 4)])
                        sT.inc(t.matmul(ps_o[0:64, q0:T],
                                        v_tm[:, D * ksub + 64 * hA:D * ksub + 64 * hA + 64],
                                        sbp[0][:, q0:T],
                                        start=(tau == 0 and ksub == 0), stop=False,
                                        tile_position=(0, 0)))
                        sT.inc(t.matmul(ps_o[64:128, q0:T],
                                        v_tm[:, D * ksub + 64 * hB:D * ksub + 64 * hB + 64],
                                        sbp[1][:, q0:T],
                                        start=(tau == 0 and ksub == 0), stop=False,
                                        tile_position=(0, 64)))
                        bread(sbp[0], sT, sT.n)
                        bread(sbp[1], sT, sT.n)
                        W(t, sV, kt_done[ksub])
                        if st_ev is not None:
                            W(t, sV, st_ev)
                        sT.inc(t.matmul(ps_state[0:64, 64 * hp:64 * hp + 64],
                                        k_tm[:, D * ksub + 128 * hp:D * ksub + 128 * hp + 64],
                                        v_tm[:, D * ksub + 64 * hA:D * ksub + 64 * hA + 64],
                                        start=(tau == 0 and hp == 0 and ksub == 0), stop=False,
                                        tile_position=(0, 0)))
                        sT.inc(t.matmul(ps_state[64:128, 64 * hp:64 * hp + 64],
                                        k_tm[:, D * ksub + 128 * hp + 64:D * ksub + 128 * hp + 128],
                                        v_tm[:, D * ksub + 64 * hB:D * ksub + 64 * hB + 64],
                                        start=(tau == 0 and hp == 0 and ksub == 0), stop=False,
                                        tile_position=(0, 64)))

                emit_score(0)
                emit_score(1)
                emit_ev(0)
                emit_score(2)
                emit_ev(1)
                emit_score(3)
                emit_ev(2)
                emit_ev(3)
                t_pair = sT.n

                @blk.vector
                def _(v, hp=hp):
                    W(v, sT, t_pair)
                    W(v, sG, v_gate)
                    bwait(v, sq_bf[hp % 2])
                    VS(v, v.tensor_tensor(z0[:, T * hp:T * (hp + 1)], ps_o[:],
                                          gate_bf[:, T * hp:T * (hp + 1)], ALU.mult))
                    sV.inc(v.tensor_tensor(sq_bf[hp % 2][:], z0[:, T * hp:T * (hp + 1)],
                                           z0[:, T * hp:T * (hp + 1)], ALU.mult))
                reads(ps_o, sV, sV.n)
                bread(gate_bf, sV, sV.n)
                z0_done[hp] = sV.n

                @blk.tensor
                def _(t, cc=hp):
                    if cc == 0:
                        twait(t, ps_m0)
                        twait(t, ps_m1)
                    W(t, sV, z0_done[cc])
                    sT.inc(t.matmul(ps_m0[0:1, :], ones128[:], z0[:, T * cc:T * (cc + 1)],
                                    start=(cc == 0), stop=(cc == NCH - 1)))
                    sT.inc(t.matmul(ps_m1[0:1, :], ones128[:], sq_bf[cc % 2][:],
                                    start=(cc == 0), stop=(cc == NCH - 1)))
                bread(sq_bf[hp % 2], sT, sT.n)
                emit_prep_spans(1)
            v_z0 = sV.n
            t_st = sT.n
            if tile == 0:
                dump(z0[:], "z0", [128, NCH * T], BF16, [(sV, v_z0)])

            @blk.vector
            def _(v):
                W(v, sT, sT.n)
                bwait(v, st_bf)
                sV.inc(v.tensor_copy(st_bf[:], ps_state[:]))
            st_ev = sV.n
            emit_prep_spans(4)
            if tile == 0:
                dump(st_bf[:], "st_bf", [128, T], BF16, [(sV, st_ev)])


            @blk.vector
            def _(v):
                W(v, sT, t_st)
                bwait(v, hlrow)
                VS(v, v.tensor_copy(hlrow[0:1, 0:T], ps_m0[0:1, :]))
                VS(v, v.tensor_copy(hlrow[0:1, T:2 * T], ps_m1[0:1, :]))
                VS(v, v.tensor_tensor(hlrow[0:1, 2 * T:3 * T], ps_m0[0:1, :],
                                      hlrow[0:1, 0:T], ALU.subtract))
                VS(v, v.tensor_tensor(hlrow[0:1, 3 * T:4 * T], ps_m1[0:1, :],
                                      hlrow[0:1, T:2 * T], ALU.subtract))
                sV.inc(v.tensor_scalar(murow[0:1, :], ps_m0[0:1, :], 1.0 / D, None, ALU.mult))
            v_str = sV.n
            reads(ps_m0, sV, v_str)
            reads(ps_m1, sV, v_str)
            v_hl = v_str

            @blk.tensor
            def _(t):
                W(t, sV, v_hl)
                twait(t, ps_m0)
                for j in range(NSUB):
                    for stat in range(2):
                        col = 4 * stat + j
                        off = T * stat + 128 * j
                        sT.inc(t.matmul(ps_m0[:, col:col + 1], hlrow[0:1, off:off + 128],
                                        onesrow[0:1, 0:1], start=True, stop=False))
                        sT.inc(t.matmul(ps_m0[:, col:col + 1], hlrow[0:1, 2 * T + off:2 * T + off + 128],
                                        onesrow[0:1, 0:1], start=False, stop=True))
                bread(hlrow, sT, sT.n)
            t_tr = sT.n

            @blk.vector
            def _(v):
                W(v, sT, t_tr)
                VS(v, v.tensor_copy(ln_sc[:, 24:32], ps_m0[:, 0:8]))
                VS(v, v.tensor_scalar(ln_sc[:, 32:36], ln_sc[:, 24:28], 1.0 / D, None, ALU.mult))
                VS(v, v.tensor_tensor(ln_sc[:, 36:40], ln_sc[:, 32:36], ln_sc[:, 32:36], ALU.mult))
                VS(v, v.scalar_tensor_tensor(ln_sc[:, 40:44], ln_sc[:, 28:32], 1.0 / D,
                                             ln_sc[:, 36:40], ALU.mult, ALU.subtract))
                VS(v, v.tensor_scalar(ln_sc[:, 40:44], ln_sc[:, 40:44], 1e-5, None, ALU.add))
                VS(v, v.tensor_scalar(ln_si[:, 4:8], ln_sc[:, 40:44].bitcast(I32), 1, None,
                                      ALU.logical_shift_right))
                VS(v, v.tensor_scalar(ln_si[:, 4:8], ln_si[:, 4:8], -1, 0x5f3759df, ALU.mult, ALU.add))
                VS(v, v.tensor_copy(ln_sc[:, 44:48], ln_si[:, 4:8].bitcast(F32)))
                for _ in range(3):
                    VS(v, v.tensor_tensor(ln_sc[:, 48:52], ln_sc[:, 44:48], ln_sc[:, 44:48], ALU.mult))
                    VS(v, v.tensor_tensor(ln_sc[:, 48:52], ln_sc[:, 48:52], ln_sc[:, 40:44], ALU.mult))
                    VS(v, v.tensor_scalar(ln_sc[:, 48:52], ln_sc[:, 48:52], -0.5, 1.5, ALU.mult, ALU.add))
                    VS(v, v.tensor_tensor(ln_sc[:, 44:48], ln_sc[:, 44:48], ln_sc[:, 48:52], ALU.mult))
            v_ln2 = sV.n
            reads(ps_m0, sV, v_ln2)
            if tile == 0:
                dump(ln_sc[:], "ln_sc", [128, 64], F32, [(sV, v_ln2)])

            emit_prep_spans(12)

            nxt = tile + 1
            s2, tau2 = divmod(nxt, NT)
            for sub in range(NSUB):
                @blk.vector
                def _(v, sub=sub):
                    W(v, sV, v_woprep)
                    VS(v, v.scalar_tensor_tensor(xld[sub][:], xld[sub][:],
                                                 ln_sc[:, 52 + sub:53 + sub],
                                                 co_bc[:], ALU.mult, ALU.add))
                v_u = sV.n

                for o in range(2):
                    bank = ps_sA if o == 0 else ps_sB

                    @blk.tensor
                    def _(t, sub=sub, o=o, bank=bank):
                        W(t, sV, v_z0)
                        if tile == 0:
                            W(t, sS, v_wo_ready)
                            W(t, sV, v_woprep)
                        twait(t, bank)
                        for cc in range(NCH):
                            sT.inc(t.matmul(bank[:],
                                            z0[:, T * cc + 128 * sub:T * cc + 128 * sub + 128],
                                            wo_sb[:, D * cc + 512 * o:D * cc + 512 * o + 512],
                                            start=(cc == 0), stop=False))

                    @blk.tensor
                    def _(t, sub=sub, o=o, bank=bank):
                        W(t, sV, v_str)
                        sT.inc(t.matmul(bank[:], murow[0:1, 128 * sub:128 * (sub + 1)],
                                        nB_bf[0:1, 512 * o:512 * (o + 1)],
                                        start=False, stop=True))
                    t_wo = sT.n

                    @blk.vector
                    def _(v, sub=sub, o=o, bank=bank):
                        W(v, sT, t_wo)
                        W(v, sV, v_ln2)
                        W(v, sV, v_u)
                        if o == 0:
                            bwait(v, x1buf[sub % 2])
                        sV.inc(v.scalar_tensor_tensor(
                            x1buf[sub % 2][:, 512 * o:512 * (o + 1)], bank[:],
                            ln_sc[:, 44 + sub:45 + sub],
                            xld[sub][:, 512 * o:512 * (o + 1)], ALU.mult, ALU.add))
                    reads(bank, sV, sV.n)
                    if o == 1:
                        bread(xld[sub], sV, sV.n)
                v_x1 = sV.n

                @blk.sync
                def _(y, sub=sub):
                    W(y, sV, v_x1)
                    sY.incd(y.dma_start(out=scr_x1[s, t0 + 128 * sub:t0 + 128 * (sub + 1), :],
                                        in_=x1buf[sub % 2][:]))
                bread(x1buf[sub % 2], sY, sY.n)

                if nxt < SPC * NT:
                    emit_x_stats(s2, T * tau2, sub)
            if nxt < SPC * NT:
                emit_ln1_newton()

            emit_prep_chunks(4)

    emit_prep_slice(10 ** 9)
    v_x1_stores = sY.n
    v_prep_q = sQ.n

    stA.close()
    EB = st.enter_context
    w1_sb = EB(nc.sbuf_tensor("w1_sb", [128, NCH * DH], BF16))
    w2_sb = EB(nc.sbuf_tensor("w2_sb", [128, 32 * D], BF16))
    g_fm = EB(nc.sbuf_tensor("g_fm", [128, 32 * T], BF16))
    xn3_tm = [EB(nc.sbuf_tensor(f"xn3_tm{i}", [128, D], BF16)) for i in range(2)]
    xn3_fm = EB(nc.sbuf_tensor("xn3_fm", [128, NCH * T], BF16))
    xbld = [EB(nc.sbuf_tensor(f"xbld{i}", [128, D], F32)) for i in range(4)]
    ybuf = [EB(nc.sbuf_tensor(f"ybuf{i}", [128, D], F32)) for i in range(2)]

    vT_A, vV_A, vS_A, vG_A = sT.n, sV.n, sS.n, sG.n

    @blk.sync
    def _(y):
        W(y, sT, vT_A)
        W(y, sV, vV_A)
        W(y, sS, vS_A)
        W(y, sG, vG_A)
        W(y, sQ, v_prep_q)
        W(y, sY, v_x1_stores)
        for cc in range(NCH):
            sY.incd(y.dma_start_transpose(w1_sb[:, DH * cc:DH * (cc + 1)],
                                          scr_w1[:, 128 * cc:128 * (cc + 1)]))
    v_w1 = sY.n

    @blk.gpsimd
    def _(g):
        W(g, sT, vT_A)
        W(g, sV, vV_A)
        W(g, sS, vS_A)
        W(g, sQ, v_prep_q)
        sQ.incd(g.dma_start(out=w2_sb[:].rearrange("p (c j) -> p c j", c=32), in_=scr_w2.rearrange("(c p) j -> p c j", p=128)))
    v_w2 = sQ.n

    @blk.vector
    def _(v):
        W(v, sV, vV_A)
        sV.inc(v.tensor_tensor(c1_sb[:], c1_sb[:], vecs[:, 18:50], ALU.add))
    v_c1 = sV.n

    def emit_xb_stats(s2, t02, sub):
        @blk.sync
        def _(y, sub=sub):
            bwait(y, xbld[sub])
            sY.incd(y.dma_start(out=xbld[sub][:],
                                in_=scr_x1[s2, t02 + 128 * sub:t02 + 128 * (sub + 1), :]))
        xv = sY.n

        @blk.vector
        def _(v, sub=sub, xv=xv):
            W(v, sY, xv)
            sV.inc(v.bn_stats(bn6[:, 0:6], xbld[sub][:, 0:512]))
            VS(v, v.bn_stats(bn6[:, 6:12], xbld[sub][:, 512:1024]))
            VS(v, v.bn_aggr(ln_sc[:, 2 * sub:2 * sub + 2], bn6[:]))

    for sub in range(NSUB):
        emit_xb_stats(0, 0, sub)
    emit_ln1_newton()

    for g8 in range(SPC * NT):
        s, tau = divmod(g8, NT)
        t0 = T * tau
        nxt = g8 + 1
        s2, tau2 = divmod(nxt, NT)

        snv = []
        for sub in range(NSUB):
            xnb = xn3_tm[sub % 2]

            @blk.vector
            def _(v, sub=sub, xnb=xnb):
                bwait(v, xnb)
                VS(v, v.tensor_scalar(xbld[sub][:], xbld[sub][:],
                                      ln_sc[:, 12 + sub:13 + sub], None, ALU.mult))
                VS(v, v.tensor_scalar(xnb[:], xbld[sub][:],
                                      ln_sc[:, 20 + sub:21 + sub], None, ALU.add))
            snv.append(sV.n)

            bank = ps_m0 if sub % 2 == 0 else ps_m1

            @blk.tensor
            def _(t, sub=sub, bank=bank, xnb=xnb):
                W(t, sV, snv[sub])
                twait(t, bank)
                pb = bank[:].bitcast(BF16)
                for cc in range(NCH):
                    sT.inc(t.transpose(pb[:, 128 * cc:128 * (cc + 1)],
                                       xnb[:, 128 * cc:128 * (cc + 1)], ident[:]))
            t_tp = sT.n
            bread(xnb, sT, sT.n)

            @blk.vector
            def _(v, sub=sub, bank=bank, t_tp=t_tp):
                W(v, sT, t_tp)
                out = xn3_fm[:].rearrange("p (c t) -> p c t", c=NCH)[:, :, 128 * sub:128 * (sub + 1)]
                sV.inc(v.tensor_copy(out, bank[:].bitcast(BF16).rearrange("p (c t) -> p c t", c=NCH)))
            reads(bank, sV, sV.n)
        v_xn3 = sV.n

        sbh = SBatch()
        hbanks = [ps_p0, ps_p1, ps_sA]
        for m in range(32):
            bank = hbanks[m % 3]

            @blk.tensor
            def _(t, m=m, bank=bank):
                W(t, sV, v_xn3)
                if g8 == 0:
                    W(t, sY, v_w1)
                    W(t, sV, v_c1)
                twait(t, bank)
                for cc in range(NCH):
                    sT.inc(t.matmul(bank[:], w1_sb[:, DH * cc + 128 * m:DH * cc + 128 * m + 128],
                                    xn3_fm[:, T * cc:T * (cc + 1)],
                                    start=(cc == 0), stop=(cc == NCH - 1)))
            t_h = sT.n

            prom = sbh.add([(sT, t_h)],
                           lambda sc, m=m, bank=bank: sc.activation(
                               g_fm[:, T * m:T * (m + 1)], bank[:], AF.Gelu,
                               bias=c1_sb[:, m:m + 1]))
            reads(bank, sS, prom)
        sbh.flush()
        v_g = sS.n

        for sub in range(NSUB):
            for o in range(2):
                bank = ps_sB if o == 0 else ps_o

                @blk.tensor
                def _(t, sub=sub, o=o, bank=bank):
                    W(t, sS, v_g)
                    if g8 == 0:
                        W(t, sQ, v_w2)
                    twait(t, bank)
                    for cc in range(32):
                        sT.inc(t.matmul(bank[:],
                                        g_fm[:, T * cc + 128 * sub:T * cc + 128 * sub + 128],
                                        w2_sb[:, D * cc + 512 * o:D * cc + 512 * o + 512],
                                        start=(cc == 0), stop=False))
                    sT.inc(t.matmul(bank[:], onesrow[:], c2_bf[0:1, 512 * o:512 * (o + 1)],
                                    start=False, stop=True))
                t_y = sT.n

                @blk.vector
                def _(v, sub=sub, o=o, bank=bank, t_y=t_y):
                    W(v, sT, t_y)
                    if o == 0:
                        bwait(v, ybuf[sub % 2])
                    sV.inc(v.scalar_tensor_tensor(
                        ybuf[sub % 2][:, 512 * o:512 * (o + 1)],
                        xbld[sub][:, 512 * o:512 * (o + 1)],
                        ln_sc[:, 52 + sub:53 + sub],
                        bank[:], ALU.mult, ALU.add))
                reads(bank, sV, sV.n)
                if o == 1:
                    bread(xbld[sub], sV, sV.n)
            v_y = sV.n

            @blk.sync
            def _(y, sub=sub):
                W(y, sV, v_y)
                sY.incd(y.dma_start(out=ap_y[s, t0 + 128 * sub:t0 + 128 * (sub + 1), :],
                                    in_=ybuf[sub % 2][:]))
            bread(ybuf[sub % 2], sY, sY.n)

            if nxt < SPC * NT:
                emit_xb_stats(s2, T * tau2, sub)
        if nxt < SPC * NT:
            emit_ln1_newton()

    if _DEBUG:
        @blk.sync
        def _(y):
            W(y, sQ, sQ.n)
            W(y, sV, sV.n)
            W(y, sT, sT.n)
            W(y, sS, sS.n)
            sY.incd(y.dma_start(out=dbg_out("scr_x1", [SPC, N, D], F32), in_=scr_x1))
            sY.incd(y.dma_start(out=dbg_out("scr_w1", [DH, D], BF16), in_=scr_w1))
            sY.incd(y.dma_start(out=dbg_out("scr_w2", [DH, D], BF16), in_=scr_w2))
            sY.incd(y.dma_start(out=dbg_out("c1_sb", [128, 32], F32), in_=c1_sb[:]))

    v_final = sY.n

    @blk.gpsimd
    def _(g):
        W(g, sY, v_final)

    st.close()
    return nc


_NC_CACHE = None


def kernel(**inputs):
    global _NC_CACHE
    x = np.ascontiguousarray(np.asarray(inputs["L0_attn_x"], np.float32))
    staged = _host_stage(inputs)
    if _NC_CACHE is None:
        _NC_CACHE = build_nc()
    nc = _NC_CACHE
    in_maps = []
    for c in range(NCORES):
        m = dict(staged)
        m["x"] = np.ascontiguousarray(x[SPC * c:SPC * (c + 1)])
        in_maps.append(m)
    res = run_bass_kernel_spmd(nc, in_maps, core_ids=list(range(NCORES)))
    out = np.concatenate([np.asarray(r["y"], np.float32) for r in res.results], axis=0)
    return out


# version 21
# speedup vs baseline: 1.7073x; 1.0441x over previous
import numpy as np
import ml_dtypes
import contextlib
import concourse.bass as bass
import concourse.mybir as mybir
from concourse.bass_utils import run_bass_kernel_spmd

F32 = mybir.dt.float32
BF16 = mybir.dt.bfloat16
I32 = mybir.dt.int32
AF = mybir.ActivationFunctionType
ALU = mybir.AluOpType

_DEBUG = False

NCORES = 8
SPC = 4
N = 2048
D = 1024
H = 16
HD = 64
DH = 4096
T = 512
NT = N // T
NSUB = T // 128
NCH = D // 128
NHP = H // 2

TWO_PI = float(np.float32(2 * np.pi))
TWO_PI_LO = float(np.float64(2 * np.pi) - np.float64(np.float32(2 * np.pi)))
INV_2PI = float(1.0 / (2 * np.pi))
PI_2 = float(np.pi / 2)


def _perm_qk():
    p1 = np.zeros(64, np.int64)
    p1[0::2] = np.arange(32)
    p1[1::2] = np.arange(32, 64)
    return np.concatenate([64 * h + p1 for h in range(H)])


def _stage_mats(w, j):
    s = 1 << j
    out = np.zeros((32, 128, 128), np.float32)
    ii = np.arange(128)
    i = ii
    a_sel = (i // s) % 2 == 0
    ia = i[a_sel]
    for c in range(32):
        gi = 128 * c + ia
        k = (gi // (2 * s)) * s + gi % s
        out[c, ia, ia] = w[j, k, 0, 0]
        out[c, ia, ia + s] = w[j, k, 0, 1]
        out[c, ia + s, ia] = w[j, k, 1, 0]
        out[c, ia + s, ia + s] = w[j, k, 1, 1]
    return out


def _big_stage_vecs(w, j, transpose2x2):
    s = 1 << j
    wS = np.zeros((32, 128), np.float32)
    wP = np.zeros((32, 128), np.float32)
    ii = np.arange(128)
    for c in range(32):
        i = 128 * c + ii
        is_a = (i // s) % 2 == 0
        ia = np.where(is_a, i, i - s)
        k = (ia // (2 * s)) * s + ia % s
        wS[c] = np.where(is_a, w[j, k, 0, 0], w[j, k, 1, 1])
        if not transpose2x2:
            wP[c] = np.where(is_a, w[j, k, 0, 1], w[j, k, 1, 0])
        else:
            wP[c] = np.where(is_a, w[j, k, 1, 0], w[j, k, 0, 1])
    return wS, wP


def _host_stage(inputs):
    f4 = np.float32
    Wq = np.asarray(inputs["L0_attn_Wq"], f4)
    Wk = np.asarray(inputs["L0_attn_Wk"], f4)
    upw = np.asarray(inputs["L0_ffn_up_weights"], f4)
    dnw = np.asarray(inputs["L0_ffn_down_weights"], f4)
    perm = _perm_qk()
    d = {}
    d["wq"] = np.ascontiguousarray(Wq[:, perm])
    d["wk"] = np.ascontiguousarray(Wk[:, perm])
    d["wv"] = np.ascontiguousarray(np.asarray(inputs["L0_attn_Wv"], f4))
    d["wo"] = np.ascontiguousarray(np.asarray(inputs["L0_attn_Wo"], f4))
    d["wg1"] = np.ascontiguousarray(np.asarray(inputs["L0_attn_Wg1"], f4))
    d["wg2"] = np.ascontiguousarray(np.asarray(inputs["L0_attn_Wg2"], f4))
    wea = np.zeros((DH, 1032), f4)
    wea[:, :D] = np.asarray(inputs["L0_ffn_W_exp"], f4).T
    wea[:, D] = np.asarray(inputs["L0_ffn_b_exp"], f4)
    d["wea"] = wea
    d["wcon"] = np.ascontiguousarray(np.asarray(inputs["L0_ffn_W_con"], f4))

    cu = np.zeros((7, 32, 128, 128), f4)
    cu[0] = _stage_mats(upw, 6).transpose(0, 2, 1)
    for idx, j in enumerate([5, 4, 3, 2, 1, 0]):
        cu[1 + idx] = _stage_mats(upw, j)
    d["comp_up"] = cu
    cd = np.zeros((7, 32, 128, 128), f4)
    cd[0] = _stage_mats(dnw, 0)
    for idx, j in enumerate([1, 2, 3, 4, 5, 6]):
        cd[1 + idx] = _stage_mats(dnw, j).transpose(0, 2, 1)
    d["comp_dn"] = cd

    wb = np.zeros((2, 5, 32, 2, 128), f4)
    for si, j in enumerate(range(7, 12)):
        wS, wP = _big_stage_vecs(upw, j, False)
        wb[0, si, :, 0, :] = wS
        wb[0, si, :, 1, :] = wP
        wS, wP = _big_stage_vecs(dnw, j, True)
        wb[1, si, :, 0, :] = wS
        wb[1, si, :, 1, :] = wP
    d["wbig"] = np.ascontiguousarray(wb.reshape(640, 128).T)

    p = np.arange(128)
    tf = (p % 64) // 2
    nv = np.zeros((128, 52), f4)
    nv[:, 0] = 10000.0 ** (-2.0 * tf / 64.0)
    nv[:, 1] = np.where(p % 2 == 0, -1.0, 1.0)
    nv[:, 2:10] = np.asarray(inputs["L0_attn_ln_w"], f4).reshape(8, 128).T
    nv[:, 10:18] = np.asarray(inputs["L0_attn_ln_b"], f4).reshape(8, 128).T
    nv[:, 18:50] = np.asarray(inputs["L0_ffn_up_bias"], f4).reshape(32, 128).T
    d["vecs"] = nv
    d["bcon_row"] = np.asarray(inputs["L0_ffn_b_con"], f4).reshape(1, D)
    d["dbias_row"] = np.asarray(inputs["L0_ffn_down_bias"], f4).reshape(1, D)
    return d


def build_nc():
    nc = bass.Bass("TRN2", target_bir_lowering=False)

    ap_x = nc.dram_tensor("x", [SPC, N, D], F32, kind="ExternalInput").ap()
    ap_wq = nc.dram_tensor("wq", [D, D], F32, kind="ExternalInput").ap()
    ap_wk = nc.dram_tensor("wk", [D, D], F32, kind="ExternalInput").ap()
    ap_wv = nc.dram_tensor("wv", [D, D], F32, kind="ExternalInput").ap()
    ap_wo = nc.dram_tensor("wo", [D, D], F32, kind="ExternalInput").ap()
    ap_wg1 = nc.dram_tensor("wg1", [D, HD], F32, kind="ExternalInput").ap()
    ap_wg2 = nc.dram_tensor("wg2", [HD, D], F32, kind="ExternalInput").ap()
    ap_wea = nc.dram_tensor("wea", [DH, 1032], F32, kind="ExternalInput").ap()
    ap_wcon = nc.dram_tensor("wcon", [DH, D], F32, kind="ExternalInput").ap()
    ap_cu = nc.dram_tensor("comp_up", [7, 32, 128, 128], F32, kind="ExternalInput").ap()
    ap_cd = nc.dram_tensor("comp_dn", [7, 32, 128, 128], F32, kind="ExternalInput").ap()
    ap_wbig = nc.dram_tensor("wbig", [128, 640], F32, kind="ExternalInput").ap()
    ap_vecs = nc.dram_tensor("vecs", [128, 52], F32, kind="ExternalInput").ap()
    ap_bcon = nc.dram_tensor("bcon_row", [1, D], F32, kind="ExternalInput").ap()
    ap_dbias = nc.dram_tensor("dbias_row", [1, D], F32, kind="ExternalInput").ap()
    ap_y = nc.dram_tensor("y", [SPC, N, D], F32, kind="ExternalOutput").ap()

    dbg = {}

    def dbg_out(name, shape, dt):
        if name not in dbg:
            dbg[name] = nc.dram_tensor("dbg_" + name, shape, dt, kind="ExternalOutput").ap()
        return dbg[name]

    def dump(buf_ap, name, shape, dt, waits):
        if not _DEBUG:
            return
        o = dbg_out(name, shape, dt)

        @blk.sync
        def _(y):
            for sem, val in waits:
                W(y, sem, val)
            sY.incd(y.dma_start(out=o, in_=buf_ap))

    scr_x1 = nc.dram_tensor("scr_x1", [SPC, N, D], F32, kind="Internal").ap()
    scr_wa = nc.dram_tensor("scr_wa", [DH, 1032], F32, kind="Internal").ap()
    scr_wb = nc.dram_tensor("scr_wb", [DH, 1032], F32, kind="Internal").ap()
    scr_wc = nc.dram_tensor("scr_wc", [DH, 1032], F32, kind="Internal").ap()
    scr_wd = nc.dram_tensor("scr_wd", [DH, 1032], F32, kind="Internal").ap()
    scr_w1 = nc.dram_tensor("scr_w1", [DH, D], BF16, kind="Internal").ap()
    scr_w2 = nc.dram_tensor("scr_w2", [DH, D], BF16, kind="Internal").ap()

    st = contextlib.ExitStack()
    E = st.enter_context

    class Sem:
        def __init__(self, name):
            self.h = E(nc.semaphore(name))
            self.n = 0

        def inc(self, inst, k=1):
            inst.then_inc(self.h, k)
            self.n += k
            return self.n

        def incd(self, inst):
            return self.inc(inst, 16)

    sY = Sem("sY")
    sQ = Sem("sQ")
    sT = Sem("sT")
    sV = Sem("sV")
    sS = Sem("sS")
    sG = Sem("sG")

    _wseen = {}

    def W(eng, sem, val):
        if val <= 0:
            return
        key = (str(eng.engine), id(sem))
        if _wseen.get(key, -1) >= val:
            return
        _wseen[key] = val
        eng.wait_ge(sem.h, val)

    def VS(v, inst):
        sV.inc(inst)
        _wseen[(str(v.engine), id(sV))] = sV.n
        v.wait_ge(sV.h, sV.n)

    vecs = E(nc.sbuf_tensor("vecs_sb", [128, 52], F32))
    ident = E(nc.sbuf_tensor("ident", [128, 128], BF16))
    ones128 = E(nc.sbuf_tensor("ones128", [128, 1], BF16))
    onesrow = E(nc.sbuf_tensor("onesrow", [1, 128], BF16))
    c2_bf = E(nc.sbuf_tensor("c2_bf", [1, D], BF16))
    c1_sb = E(nc.sbuf_tensor("c1_sb", [128, 32], F32))
    ln_sc = E(nc.sbuf_tensor("ln_sc", [128, 64], F32))
    ln_si = E(nc.sbuf_tensor("ln_si", [128, 8], I32))
    bn6 = E(nc.sbuf_tensor("bn6", [128, 12], F32))

    ps_state = E(nc.psum_tensor("ps_state", [128, 512], F32))
    ps_sA = E(nc.psum_tensor("ps_sA", [128, 512], F32))
    ps_sB = E(nc.psum_tensor("ps_sB", [128, 512], F32))
    ps_o = E(nc.psum_tensor("ps_o", [128, 512], F32))
    ps_p0 = E(nc.psum_tensor("ps_p0", [128, 512], F32))
    ps_p1 = E(nc.psum_tensor("ps_p1", [128, 512], F32))
    ps_m0 = E(nc.psum_tensor("ps_m0", [128, 512], F32))
    ps_m1 = E(nc.psum_tensor("ps_m1", [128, 512], F32))

    guard = {}

    def twait(t, bank):
        for sem, val in guard.pop(id(bank), []):
            W(t, sem, val)

    def reads(bank, sem, val):
        guard.setdefault(id(bank), []).append((sem, val))

    blk = E(nc.Block())

    stA = contextlib.ExitStack()
    EA = stA.enter_context
    wq_sb = EA(nc.sbuf_tensor("wq_sb", [128, NCH * D], BF16))
    wk_sb = EA(nc.sbuf_tensor("wk_sb", [128, NCH * D], BF16))
    wv_sb = EA(nc.sbuf_tensor("wv_sb", [128, NCH * D], BF16))
    wo_sb = EA(nc.sbuf_tensor("wo_sb", [128, NCH * D], BF16))
    wg1_sb = EA(nc.sbuf_tensor("wg1_sb", [128, NCH * HD], BF16))
    wg2_sb = EA(nc.sbuf_tensor("wg2_sb", [64, D], BF16))
    wbig = EA(nc.sbuf_tensor("wbig_sb", [128, 640], F32))
    cos_t = EA(nc.sbuf_tensor("cos_t", [128, N], BF16))
    ssin_t = EA(nc.sbuf_tensor("ssin_t", [128, N], BF16))
    tri_bf = EA(nc.sbuf_tensor("tri_bf", [128, 128], BF16))
    lnb_bf = EA(nc.sbuf_tensor("lnb_bf", [128, 8], BF16))
    co_bc = EA(nc.sbuf_tensor("co_bc", [128, D], F32))
    corow_bf = EA(nc.sbuf_tensor("corow_bf", [1, D], BF16))
    nB_bf = EA(nc.sbuf_tensor("nB_bf", [1, D], BF16))
    murow = EA(nc.sbuf_tensor("murow", [1, T], BF16))
    hlrow = EA(nc.sbuf_tensor("hlrow", [1, 4 * T], BF16))

    st0 = contextlib.ExitStack()
    E0 = st0.enter_context
    it_buf = E0(nc.sbuf_tensor("it_buf", [128, N], I32))
    it_buf2 = E0(nc.sbuf_tensor("it_buf2", [128, N], I32))
    tb0 = E0(nc.sbuf_tensor("tb0", [128, N], F32))
    tb1 = E0(nc.sbuf_tensor("tb1", [128, N], F32))
    wof32 = [E0(nc.sbuf_tensor(f"wof32_{i}", [128, D], F32)) for i in range(2)]
    wor_bf = E0(nc.sbuf_tensor("wor_bf", [128, D], BF16))
    rowa = E0(nc.sbuf_tensor("rowa", [1, D], F32))
    rowb = E0(nc.sbuf_tensor("rowb", [1, D], F32))
    c2row = E0(nc.sbuf_tensor("c2row", [1, D], F32))

    v_wq = v_wk = v_wv = v_wg = 0

    @blk.sync
    def _(y):
        sY.incd(y.dma_start(out=vecs[:], in_=ap_vecs))
        sY.incd(y.dma_start(out=wbig[:], in_=ap_wbig))
        sY.incd(y.dma_start(out=rowa[:], in_=ap_bcon))
        sY.incd(y.dma_start(out=rowb[:], in_=ap_dbias))
    v_rows = sY.n

    @blk.gpsimd
    def _(g):
        nonlocal_marker = None
        sQ.incd(g.dma_start(out=wq_sb[:].rearrange("p (c j) -> p c j", c=NCH), in_=ap_wq.rearrange("(c p) j -> p c j", p=128)))
        sQ.incd(g.dma_start(out=wk_sb[:].rearrange("p (c j) -> p c j", c=NCH), in_=ap_wk.rearrange("(c p) j -> p c j", p=128)))
        sQ.incd(g.dma_start(out=wv_sb[:].rearrange("p (c j) -> p c j", c=NCH), in_=ap_wv.rearrange("(c p) j -> p c j", p=128)))
        sQ.incd(g.dma_start(out=wg1_sb[:].rearrange("p (c j) -> p c j", c=NCH), in_=ap_wg1.rearrange("(c p) j -> p c j", p=128)))
        sQ.incd(g.dma_start(out=wg2_sb[:], in_=ap_wg2))
    v_wload = sQ.n

    @blk.gpsimd
    def _(g):
        sG.inc(g.memset(ones128[:], 1.0))
        sG.inc(g.memset(onesrow[:], 1.0))
        sG.inc(g.iota(it_buf[:, 0:128], [[1, 128]], base=0, channel_multiplier=-1))
    v_iota1 = sG.n

    @blk.vector
    def _(v):
        W(v, sG, v_iota1)
        sV.inc(v.tensor_scalar(ident[:], it_buf[:, 0:128], 0, None, ALU.is_equal))
        sV.inc(v.tensor_scalar(tri_bf[:], it_buf[:, 0:128], 0, None, ALU.is_ge))
        W(v, sY, v_rows)
        VS(v, v.tensor_tensor(c2row[:], rowa[:], rowb[:], ALU.add))
        VS(v, v.tensor_copy(c2_bf[:], c2row[:]))
        sV.inc(v.tensor_copy(lnb_bf[:], vecs[:, 10:18]))
    v_masks = sV.n

    @blk.gpsimd
    def _(g):
        W(g, sV, v_masks)
        sG.inc(g.iota(it_buf[:], [[1, N]], base=0, channel_multiplier=0))
    v_iota2 = sG.n

    def emit_table(shift, out_tile, signed, prev_ss):
        @blk.vector
        def _(v):
            W(v, sG, v_iota2)
            W(v, sY, v_rows)
            W(v, sS, prev_ss)
            VS(v, v.tensor_scalar(tb0[:], it_buf[:], vecs[:, 0:1], None, ALU.mult))
            if shift:
                VS(v, v.tensor_scalar(tb0[:], tb0[:], shift, None, ALU.add))
            VS(v, v.tensor_scalar(tb1[:], tb0[:], INV_2PI, None, ALU.mult))
            VS(v, v.tensor_copy(it_buf2[:], tb1[:]))
            VS(v, v.tensor_copy(tb1[:], it_buf2[:]))
            VS(v, v.scalar_tensor_tensor(tb0[:], tb1[:], -TWO_PI, tb0[:], ALU.mult, ALU.add))
            VS(v, v.scalar_tensor_tensor(tb0[:], tb1[:], -TWO_PI_LO, tb0[:], ALU.mult, ALU.add))
        vv = sV.n

        @blk.scalar
        def _(s):
            W(s, sV, vv)
            if signed:
                sS.inc(s.activation(tb1[:], tb0[:], AF.Sin))
            else:
                sS.inc(s.activation(out_tile[:], tb0[:], AF.Sin))
        ss = sS.n
        if signed:
            @blk.vector
            def _(v):
                W(v, sS, ss)
                sV.inc(v.tensor_scalar(out_tile[:], tb1[:], vecs[:, 1:2], None, ALU.mult))

    emit_table(PI_2, cos_t, False, 0)
    emit_table(0.0, ssin_t, True, sS.n)
    v_tab_s, v_tab_v = sS.n, sV.n

    wo_guard = {}
    co_t = 0
    for c in range(NCH):
        @blk.sync
        def _(y, c=c):
            if c >= 2:
                W(y, sS, wo_guard[c - 2])
            sY.incd(y.dma_start(out=wof32[c % 2][:], in_=ap_wo[128 * c:128 * c + 128, :]))
        v_ld = sY.n

        @blk.scalar
        def _(s, c=c, co_t=co_t):
            W(s, sY, v_ld)
            W(s, sV, v_masks)
            W(s, sT, co_t)
            sS.inc(s.activation(wor_bf[:], wof32[c % 2][:], AF.Identity))
            sS.inc(s.activation(wo_sb[:, D * c:D * (c + 1)], wof32[c % 2][:], AF.Identity,
                                scale=vecs[:, 2 + c:3 + c]))
        wo_guard[c] = sS.n
        v_wor = sS.n

        @blk.tensor
        def _(t, c=c):
            W(t, sS, v_wor)
            for o in range(2):
                bank = ps_m0 if o == 0 else ps_m1
                if c == 0:
                    twait(t, bank)
                sT.inc(t.matmul(bank[0:1, :], lnb_bf[:, c:c + 1], wor_bf[:, 512 * o:512 * (o + 1)],
                                start=(c == 0), stop=(c == NCH - 1)))
        co_t = sT.n
    v_comm = sT.n
    v_wo_ready = wo_guard[NCH - 1]

    @blk.vector
    def _(v):
        W(v, sT, v_comm)
        sV.inc(v.tensor_copy(corow_bf[0:1, 0:512], ps_m0[0:1, :]))
        sV.inc(v.tensor_copy(corow_bf[0:1, 512:1024], ps_m1[0:1, :]))
    v_corow = sV.n
    reads(ps_m0, sV, v_corow)
    reads(ps_m1, sV, v_corow)

    @blk.tensor
    def _(t):
        W(t, sS, wo_guard[NCH - 1])
        for o in range(2):
            bank = ps_p0 if o == 0 else ps_p1
            twait(t, bank)
            for c in range(NCH):
                sT.inc(t.matmul(bank[0:1, :], ones128[:], wo_sb[:, D * c + 512 * o:D * c + 512 * o + 512],
                                start=(c == 0), stop=(c == NCH - 1)))
        W(t, sV, v_corow)
        for o in range(2):
            bank = ps_m0 if o == 0 else ps_m1
            twait(t, bank)
            sT.inc(t.matmul(bank[:], onesrow[:], corow_bf[0:1, 512 * o:512 * (o + 1)],
                            start=True, stop=True))
    v_nbmm = sT.n

    @blk.vector
    def _(v):
        W(v, sT, v_nbmm)
        sV.inc(v.tensor_scalar(nB_bf[0:1, 0:512], ps_p0[0:1, :], -1.0, None, ALU.mult))
        sV.inc(v.tensor_scalar(nB_bf[0:1, 512:1024], ps_p1[0:1, :], -1.0, None, ALU.mult))
        sV.inc(v.tensor_copy(co_bc[:, 0:512], ps_m0[:]))
        sV.inc(v.tensor_copy(co_bc[:, 512:1024], ps_m1[:]))
    v_woprep = sV.n
    for b in (ps_p0, ps_p1, ps_m0, ps_m1):
        reads(b, sV, v_woprep)

    if _DEBUG:
        dump(co_bc[:], "co_bc", [128, D], F32, [(sV, v_woprep)])
        dump(nB_bf[:], "nB_bf", [1, D], BF16, [(sV, v_woprep)])
        dump(wo_sb[:], "wo_sb", [128, NCH * D], BF16, [(sS, v_wo_ready)])

    st0.close()
    xld = [EA(nc.sbuf_tensor(f"xld{i}", [128, D], F32)) for i in range(4)]
    xn_tm = [EA(nc.sbuf_tensor(f"xn_tm{i}", [128, D], BF16)) for i in range(2)]
    xn_fm = EA(nc.sbuf_tensor("xn_fm", [128, NCH * T], BF16))
    z0 = xn_fm
    q_fm = EA(nc.sbuf_tensor("q_fm", [128, NCH * T], BF16))
    k_fm = EA(nc.sbuf_tensor("k_fm", [128, NCH * T], BF16))
    v_tm = EA(nc.sbuf_tensor("v_tm", [128, NSUB * D], BF16))
    k_tm = EA(nc.sbuf_tensor("k_tm", [128, NSUB * D], BF16))
    qsil = [EA(nc.sbuf_tensor(f"qsil{i}", [128, T], BF16)) for i in range(2)]
    qshuf = [EA(nc.sbuf_tensor(f"qshuf{i}", [128, T], BF16)) for i in range(2)]
    qt1 = [EA(nc.sbuf_tensor(f"qt1_{i}", [128, T], BF16)) for i in range(2)]
    s_bf = [[EA(nc.sbuf_tensor(f"s_bf{p}_{i}", [128, T], BF16)) for i in range(2)]
            for p in range(2)]
    st_bf = EA(nc.sbuf_tensor("st_bf", [128, T], BF16))
    g1_bf = EA(nc.sbuf_tensor("g1_bf", [64, T], BF16))
    gth = [EA(nc.sbuf_tensor(f"gth{i}", [128, T], BF16)) for i in range(2)]
    gate_bf = EA(nc.sbuf_tensor("gate_bf", [128, NCH * T], BF16))
    sq_bf = [EA(nc.sbuf_tensor(f"sq_bf{i}", [128, T], BF16)) for i in range(2)]
    x1buf = [EA(nc.sbuf_tensor(f"x1buf{i}", [128, D], F32)) for i in range(2)]
    SPW = 258
    spn = [EA(nc.sbuf_tensor(f"spn{i}", [128, 4 * SPW], F32)) for i in range(2)]
    spt = EA(nc.sbuf_tensor("spt", [128, SPW], F32))
    NLANE = 2
    cmp_in = [EA(nc.sbuf_tensor(f"cmp_in{l}", [128, 7 * 128], BF16)) for l in range(NLANE)]
    cmp_ab = [[EA(nc.sbuf_tensor(f"cmp_ab{l}_{i}", [128, 128], BF16)) for i in range(2)]
              for l in range(NLANE)]
    app_in = [EA(nc.sbuf_tensor(f"app_in{l}", [128, 1032], BF16)) for l in range(NLANE)]
    app_out = EA(nc.sbuf_tensor("app_out", [128, 1032], BF16))

    def prep_units():
        spans = []
        chunks = []
        prereq = {}
        for net, src0 in ((0, ap_wea), (1, ap_wcon)):
            wsrc = 1032 if net == 0 else D
            if net == 0:
                passes = [
                    (src0, scr_wa, [(c, c + 1, c + 2, c + 3) for c in range(0, 32, 4)], [0, 1]),
                    (scr_wa, scr_wb, [(c, c + 4, c + 8, c + 12)
                                      for base in (0, 16) for c in range(base, base + 4)], [2, 3]),
                    (scr_wb, scr_wa, [(c, c + 16) for c in range(16)], [4]),
                ]
            else:
                passes = [
                    (src0, scr_wc, [(c, c + 16) for c in range(16)], [4]),
                    (scr_wc, scr_wd, [(c, c + 4, c + 8, c + 12)
                                      for base in (0, 16) for c in range(base, base + 4)], [3, 2]),
                    (scr_wd, scr_wc, [(c, c + 1, c + 2, c + 3) for c in range(0, 32, 4)], [1, 0]),
                ]
            for src, dst, groups, stages in passes:
                for grp in groups:
                    for h in range(4):
                        col0 = SPW * h
                        cw = min(SPW, wsrc - col0)
                        if cw > 0:
                            spans.append(("span", net, src, dst, grp, stages, col0, cw))
            prereq[net] = len(spans)
            for c in range(32):
                chunks.append(("chunk", net, wsrc, c, prereq[net]))
        return spans, chunks

    span_state = {"prev": [], "n": 0}

    def emit_span(net, src, dst, grp, stages, col0, cw):
        nchk = len(grp)
        A, B = spn[0], spn[1]

        @blk.gpsimd
        def _(g):
            for s_, v_ in span_state["prev"]:
                W(g, s_, v_)
            for i, c in enumerate(grp):
                sQ.incd(g.dma_start(out=A[:, SPW * i:SPW * i + cw],
                                    in_=src[128 * c:128 * c + 128, col0:col0 + cw]))
        v_ld = sQ.n
        cur, nxt = A, B
        use_g = False
        span_state["n"] += 1
        dsem = sG if use_g else sV

        def emit_ops(e):
            def SW(inst):
                dsem.inc(inst)
                _wseen[(str(e.engine), id(dsem))] = dsem.n
                e.wait_ge(dsem.h, dsem.n)
            cur2, nxt2 = cur, nxt
            for stj in stages:
                for oi in range(nchk):
                    c_out = grp[oi]
                    dist = 1 << stj
                    pi = grp.index(c_out ^ dist)
                    wcol_s = ((net * 5 + stj) * 32 + c_out) * 2
                    wcol_p = wcol_s + 1
                    W(e, sQ, v_ld)
                    SW(e.tensor_scalar(spt[:, 0:cw], cur2[:, SPW * pi:SPW * pi + cw],
                                       wbig[:, wcol_p:wcol_p + 1], None, ALU.mult))
                    SW(e.scalar_tensor_tensor(nxt2[:, SPW * oi:SPW * oi + cw],
                                              cur2[:, SPW * oi:SPW * oi + cw],
                                              wbig[:, wcol_s:wcol_s + 1],
                                              spt[:, 0:cw], ALU.mult, ALU.add))
                cur2, nxt2 = nxt2, cur2
            return cur2

        fin = [None]
        if use_g:
            @blk.gpsimd
            def _(g):
                fin[0] = emit_ops(g)
        else:
            @blk.vector
            def _(v):
                fin[0] = emit_ops(v)
        v_done = (dsem, dsem.n)
        cur = fin[0]

        @blk.gpsimd
        def _(g):
            W(g, v_done[0], v_done[1])
            for i, c in enumerate(grp):
                sQ.incd(g.dma_start(out=dst[128 * c:128 * c + 128, col0:col0 + cw],
                                    in_=cur[:, SPW * i:SPW * i + cw]))
        span_state["prev"] = [(sQ, sQ.n), v_done]

    lane_banks = [(ps_m0, ps_m1), (ps_p0, ps_p1)]
    chunk_prev = [{"q": 0, "v": 0} for _ in range(NLANE)]

    def emit_chunk_loads(units):
        n = len(units)
        lds = [0] * n
        for li, (net, wsrc, c) in enumerate(units):
            @blk.gpsimd
            def _(g, li=li, net=net, wsrc=wsrc, c=c):
                capp = ap_cu if net == 0 else ap_cd
                srcw = scr_wa if net == 0 else scr_wc
                if chunk_prev[li]["v"]:
                    W(g, sV, chunk_prev[li]["v"])
                for j in range(7):
                    sQ.incd(g.dma_start(out=cmp_in[li][:, 128 * j:128 * (j + 1)], in_=capp[j, c]))
                sQ.incd(g.dma_start(out=app_in[li][:, 0:wsrc],
                                    in_=srcw[128 * c:128 * c + 128, 0:wsrc]))
            lds[li] = sQ.n
        return lds

    def emit_chunk_compute(units, lds):
        n = len(units)
        vvs = [0] * n
        curs = [cmp_in[li][:, 0:128] for li in range(n)]
        for step in range(6):
            for li in range(n):
                bank0 = lane_banks[li][0]

                @blk.tensor
                def _(t, li=li, step=step, bank0=bank0, cur=curs[li], vv=vvs[li]):
                    W(t, sQ, lds[li])
                    if vv:
                        W(t, sV, vv)
                    twait(t, bank0)
                    sT.inc(t.matmul(bank0[:, 0:128],
                                    cmp_in[li][:, 128 * (step + 1):128 * (step + 2)],
                                    cur, start=True, stop=True))
                tt = sT.n

                @blk.vector
                def _(v, li=li, step=step, tt=tt, bank0=bank0):
                    W(v, sT, tt)
                    sV.inc(v.tensor_copy(cmp_ab[li][step % 2][:], bank0[:, 0:128]))
                vvs[li] = sV.n
                reads(bank0, sV, vvs[li])
                curs[li] = cmp_ab[li][step % 2][:]
        aps = [0] * n
        for li in range(n):
            bank0, bank1 = lane_banks[li]

            @blk.tensor
            def _(t, li=li, bank0=bank0, bank1=bank1):
                W(t, sV, vvs[li])
                twait(t, bank0)
                twait(t, bank1)
                sT.inc(t.matmul(bank0[:], curs[li], app_in[li][:, 0:512], start=True, stop=True))
                sT.inc(t.matmul(bank1[:], curs[li], app_in[li][:, 512:1024], start=True, stop=True))
            aps[li] = sT.n
        for li, (net, wsrc, c) in enumerate(units):
            bank0, bank1 = lane_banks[li]

            @blk.vector
            def _(v, li=li, bank0=bank0, bank1=bank1):
                W(v, sT, aps[li])
                bwait(v, app_out)
                sV.inc(v.tensor_copy(app_out[:, 0:512], bank0[:]))
                sV.inc(v.tensor_copy(app_out[:, 512:1024], bank1[:]))
            ev = sV.n
            reads(bank0, sV, ev)
            reads(bank1, sV, ev)
            if net == 0:
                @blk.tensor
                def _(t, li=li, ev=ev, bank1=bank1):
                    W(t, sV, ev)
                    twait(t, bank1)
                    sT.inc(t.matmul(bank1[:, 0:8], curs[li], app_in[li][:, 1024:1032],
                                    start=True, stop=True))
                t_b = sT.n

                @blk.vector
                def _(v, c=c, t_b=t_b, bank1=bank1):
                    W(v, sT, t_b)
                    sV.inc(v.tensor_copy(c1_sb[:, c:c + 1], bank1[:, 0:1]))
                reads(bank1, sV, sV.n)
                ev = sV.n
            dstw = scr_w1 if net == 0 else scr_w2

            @blk.gpsimd
            def _(g, ev=ev, c=c, dstw=dstw):
                W(g, sV, ev)
                sQ.incd(g.dma_start(out=dstw[128 * c:128 * c + 128, :], in_=app_out[:, 0:1024]))
            bread(app_out, sQ, sQ.n)
            chunk_prev[li]["v"] = ev
            chunk_prev[li]["q"] = sQ.n

    PREP_SPANS, PREP_CHUNKS = prep_units()
    span_i = [0]
    chunk_i = [0]
    prep_pend = [None]

    def emit_prep_spans(k):
        while k > 0 and span_i[0] < len(PREP_SPANS):
            u = PREP_SPANS[span_i[0]]
            span_i[0] += 1
            k -= 1
            emit_span(u[1], u[2], u[3], u[4], u[5], u[6], u[7])

    def emit_prep_chunks(k):
        while k > 0:
            if prep_pend[0] is not None:
                units, lds = prep_pend[0]
                prep_pend[0] = None
                emit_chunk_compute(units, lds)
                k -= len(units)
                continue
            grp = []
            while (len(grp) < NLANE and chunk_i[0] < len(PREP_CHUNKS)
                   and PREP_CHUNKS[chunk_i[0]][4] <= span_i[0]):
                grp.append(PREP_CHUNKS[chunk_i[0]][1:4])
                chunk_i[0] += 1
            if not grp:
                return
            prep_pend[0] = (grp, emit_chunk_loads(grp))
            if k <= len(grp):
                return

    def emit_prep_slice(k):
        emit_prep_spans(k)
        emit_prep_chunks(k)

    bufg = {}

    def bwait(eng, buf):
        for sem, val in bufg.pop(id(buf), []):
            W(eng, sem, val)

    def bread(buf, sem, val):
        bufg.setdefault(id(buf), []).append((sem, val))

    def bpop(buf):
        return bufg.pop(id(buf), [])

    class SBatch:

        def __init__(self):
            self.q = []
            self.virt = 0

        def add(self, waits, fn):
            self.q.append((list(waits), fn))
            self.virt += 1
            return sS.n + self.virt

        def flush(self):
            if not self.q:
                return
            q, promised = self.q, sS.n + self.virt
            self.q, self.virt = [], 0

            @blk.scalar
            def _(sc):
                for waits, fn in q:
                    for sem, val in waits:
                        W(sc, sem, val)
                    sS.inc(fn(sc))
            assert sS.n == promised, (sS.n, promised)

    def emit_x_stats(s2, t02, sub):
        @blk.sync
        def _(y, sub=sub):
            bwait(y, xld[sub])
            sY.incd(y.dma_start(out=xld[sub][:],
                                in_=ap_x[s2, t02 + 128 * sub:t02 + 128 * (sub + 1), :]))
        xv = sY.n

        @blk.vector
        def _(v, sub=sub, xv=xv):
            W(v, sY, xv)
            sV.inc(v.bn_stats(bn6[:, 0:6], xld[sub][:, 0:512]))
            VS(v, v.bn_stats(bn6[:, 6:12], xld[sub][:, 512:1024]))
            VS(v, v.bn_aggr(ln_sc[:, 2 * sub:2 * sub + 2], bn6[:]))

    def emit_ln1_newton():
        @blk.vector
        def _(v):
            mean = ln_sc[:, 0:8].rearrange("p (s two) -> p s two", two=2)[:, :, 0]
            var = ln_sc[:, 0:8].rearrange("p (s two) -> p s two", two=2)[:, :, 1]
            VS(v, v.tensor_scalar(ln_sc[:, 8:12], var, 1e-5, None, ALU.add))
            VS(v, v.tensor_scalar(ln_si[:, 0:4], ln_sc[:, 8:12].bitcast(I32), 1, None,
                                  ALU.logical_shift_right))
            VS(v, v.tensor_scalar(ln_si[:, 0:4], ln_si[:, 0:4], -1, 0x5f3759df,
                                  ALU.mult, ALU.add))
            VS(v, v.tensor_copy(ln_sc[:, 12:16], ln_si[:, 0:4].bitcast(F32)))
            for _ in range(3):
                VS(v, v.tensor_tensor(ln_sc[:, 16:20], ln_sc[:, 12:16], ln_sc[:, 12:16], ALU.mult))
                VS(v, v.tensor_tensor(ln_sc[:, 16:20], ln_sc[:, 16:20], ln_sc[:, 8:12], ALU.mult))
                VS(v, v.tensor_scalar(ln_sc[:, 16:20], ln_sc[:, 16:20], -0.5, 1.5,
                                      ALU.mult, ALU.add))
                VS(v, v.tensor_tensor(ln_sc[:, 12:16], ln_sc[:, 12:16], ln_sc[:, 16:20], ALU.mult))
            VS(v, v.tensor_tensor(ln_sc[:, 20:24], mean, ln_sc[:, 12:16], ALU.mult))
            VS(v, v.tensor_scalar(ln_sc[:, 20:24], ln_sc[:, 20:24], -1.0, None, ALU.mult))
            VS(v, v.tensor_tensor(ln_sc[:, 52:56], ln_sc[:, 12:16], ln_sc[:, 8:12], ALU.mult))

    st_ev = None
    for sub in range(NSUB):
        emit_x_stats(0, 0, sub)
    emit_ln1_newton()

    for s in range(SPC):
        for tau in range(NT):
            t0 = T * tau
            tile = s * NT + tau

            sn = []
            sbe = SBatch()
            for sub in range(NSUB):
                xnb = xn_tm[sub % 2]

                @blk.vector
                def _(v, sub=sub, xnb=xnb):
                    bwait(v, xnb)
                    VS(v, v.tensor_scalar(xld[sub][:], xld[sub][:],
                                          ln_sc[:, 12 + sub:13 + sub], None, ALU.mult))
                    VS(v, v.tensor_scalar(xnb[:], xld[sub][:],
                                          ln_sc[:, 20 + sub:21 + sub], None, ALU.add))
                sn.append(sV.n)
                if tile == 0 and sub == 0:
                    dump(xnb[:], "xn_tm0", [128, D], BF16, [(sV, sV.n)])

                bank = ps_m0 if sub % 2 == 0 else ps_m1

                @blk.tensor
                def _(t, sub=sub, bank=bank, xnb=xnb):
                    W(t, sV, sn[sub])
                    twait(t, bank)
                    pb = bank[:].bitcast(BF16)
                    for cc in range(NCH):
                        sT.inc(t.transpose(pb[:, 128 * cc:128 * (cc + 1)],
                                           xnb[:, 128 * cc:128 * (cc + 1)], ident[:]))
                t_tp = sT.n
                bread(xnb, sT, sT.n)

                def _evac(sc, sub=sub, bank=bank):
                    out = xn_fm[:].rearrange("p (c t) -> p c t", c=NCH)[:, :, 128 * sub:128 * (sub + 1)]
                    return sc.activation(out, bank[:].bitcast(BF16).rearrange("p (c t) -> p c t", c=NCH), AF.Identity)
                prom = sbe.add([(sT, t_tp)], _evac)
                reads(bank, sS, prom)
            sbe.flush()
            v_xnfm = sS.n
            if tile == 0:
                dump(xn_fm[:], "xn_fm", [128, NCH * T], BF16, [(sS, v_xnfm)])

            vv_done = {}
            sbv = SBatch()
            def proj_rope(dst_fm, w_sb, wload_v):
                sb = SBatch()
                mdone = []
                for m in range(NCH):
                    bank = ps_p0 if m % 2 == 0 else ps_p1
                    qs, qh, t1 = qsil[m % 2], qshuf[m % 2], qt1[m % 2]

                    @blk.tensor
                    def _(t, m=m, bank=bank):
                        W(t, sS, v_xnfm)
                        if tile == 0:
                            W(t, sQ, wload_v)
                        twait(t, bank)
                        for cc in range(NCH):
                            sT.inc(t.matmul(bank[:], w_sb[:, D * cc + 128 * m:D * cc + 128 * m + 128],
                                            xn_fm[:, T * cc:T * (cc + 1)],
                                            start=(cc == 0), stop=(cc == NCH - 1)))
                    t_mm = sT.n

                    s_si = sb.add([(sT, t_mm)] + bpop(qs),
                                  lambda sc, bank=bank, qs=qs: sc.activation(qs[:], bank[:], AF.Silu))
                    reads(bank, sS, s_si)

                    @blk.vector
                    def _(v, m=m, s_si=s_si, qs=qs, qh=qh):
                        W(v, sS, s_si)
                        bwait(v, qh)
                        mask = []
                        for i in range(0, 32, 2):
                            mask += [i + 1, i]
                        sV.inc(v.stream_shuffle(qh[:], qs[:], mask))
                    v_sh = sV.n

                    @blk.gpsimd
                    def _(g, m=m, s_si=s_si, v_sh=v_sh, qs=qs, qh=qh, t1=t1):
                        def SW(inst):
                            sG.inc(inst)
                            _wseen[(str(g.engine), id(sG))] = sG.n
                            g.wait_ge(sG.h, sG.n)
                        W(g, sS, s_si)
                        W(g, sV, v_sh)
                        if tile == 0:
                            W(g, sS, v_tab_s)
                            W(g, sV, v_tab_v)
                        bwait(g, t1)
                        SW(g.tensor_tensor(t1[:], qs[:],
                                           cos_t[:, t0:t0 + T], ALU.mult))
                        SW(g.tensor_tensor(qh[:], qh[:],
                                           ssin_t[:, t0:t0 + T], ALU.mult))
                        sG.inc(g.tensor_tensor(dst_fm[:, T * m:T * (m + 1)], t1[:],
                                               qh[:], ALU.add))
                    g_mu = sG.n
                    bread(qs, sG, g_mu)
                    bread(t1, sG, g_mu)
                    bread(qh, sG, g_mu)
                    mdone.append(g_mu)
                sb.flush()
                return mdone

            q_done = proj_rope(q_fm, wq_sb, v_wload)
            v_q = q_done[-1]
            if tile == 0:
                dump(q_fm[:], "q_fm", [128, NCH * T], BF16, [(sG, v_q)])
            k_done = proj_rope(k_fm, wk_sb, v_wload)
            v_k = k_done[-1]
            if tile == 0:
                dump(k_fm[:], "k_fm", [128, NCH * T], BF16, [(sG, v_k)])

            for sub in range(NSUB):
                for o in range(2):
                    bank = ps_p0 if (2 * sub + o) % 2 == 0 else ps_p1

                    @blk.tensor
                    def _(t, sub=sub, o=o, bank=bank):
                        W(t, sS, v_xnfm)
                        if tile == 0:
                            W(t, sQ, v_wload)
                        twait(t, bank)
                        for cc in range(NCH):
                            sT.inc(t.matmul(bank[:],
                                            xn_fm[:, T * cc + 128 * sub:T * cc + 128 * sub + 128],
                                            wv_sb[:, D * cc + 512 * o:D * cc + 512 * o + 512],
                                            start=(cc == 0), stop=(cc == NCH - 1)))
                    t_mm = sT.n

                    prom = sbv.add([(sT, t_mm)],
                                   lambda sc, sub=sub, o=o, bank=bank: sc.activation(
                                       v_tm[:, D * sub + 512 * o:D * sub + 512 * o + 512],
                                       bank[:], AF.Identity))
                    reads(bank, sS, prom)
                    vv_done[(sub, o)] = prom
            sbv.flush()
            v_v = sS.n
            if tile == 0:
                dump(v_tm[:], "v_tm", [128, NSUB * D], BF16, [(sS, v_v)])

            @blk.tensor
            def _(t):
                W(t, sS, v_xnfm)
                if tile == 0:
                    W(t, sQ, v_wload)
                twait(t, ps_p0)
                for cc in range(NCH):
                    sT.inc(t.matmul(ps_p0[0:64, :], wg1_sb[:, 64 * cc:64 * (cc + 1)],
                                    xn_fm[:, T * cc:T * (cc + 1)],
                                    start=(cc == 0), stop=(cc == NCH - 1)))
            t_g1 = sT.n

            sbg = SBatch()
            v_g1 = sbg.add([(sT, t_g1)],
                           lambda sc: sc.activation(g1_bf[:], ps_p0[0:64, :], AF.Identity))
            reads(ps_p0, sS, v_g1)
            for m in range(NCH):
                bank = ps_m0 if m % 2 == 0 else ps_m1
                gt = gth[m % 2]

                @blk.tensor
                def _(t, m=m, bank=bank):
                    W(t, sS, v_g1)
                    twait(t, bank)
                    sT.inc(t.matmul(bank[:], wg2_sb[0:64, 128 * m:128 * (m + 1)], g1_bf[:],
                                    start=True, stop=True))
                t_G = sT.n

                s_th = sbg.add([(sT, t_G)] + bpop(gt),
                               lambda sc, bank=bank, gt=gt: sc.activation(gt[:], bank[:],
                                                                          AF.Tanh, scale=0.5))
                reads(bank, sS, s_th)

                @blk.gpsimd
                def _(g, m=m, s_th=s_th, gt=gt):
                    W(g, sS, s_th)
                    if m == 0:
                        bwait(g, gate_bf)
                    sG.inc(g.tensor_scalar(gate_bf[:, T * m:T * (m + 1)], gt[:],
                                           0.5, 0.5, ALU.mult, ALU.add))
                bread(gt, sG, sG.n)
            sbg.flush()
            v_gate = sG.n
            if tile == 0:
                dump(gate_bf[:], "gate_bf", [128, NCH * T], BF16, [(sG, v_gate)])

            kt_done = {}
            sbk = SBatch()
            for sub in range(NSUB):
                bank = ps_m0 if sub % 2 == 0 else ps_m1

                @blk.tensor
                def _(t, sub=sub, bank=bank):
                    W(t, sG, v_k)
                    W(t, sG, v_gate)
                    twait(t, bank)
                    pb = bank[:].bitcast(BF16)
                    for hp in range(NHP):
                        sT.inc(t.transpose(pb[:, 128 * hp:128 * (hp + 1)],
                                           k_fm[:, T * hp + 128 * sub:T * hp + 128 * (sub + 1)],
                                           ident[:]))
                t_tp = sT.n

                prom = sbk.add([(sT, t_tp)],
                               lambda sc, sub=sub, bank=bank: sc.activation(
                                   k_tm[:, D * sub:D * (sub + 1)], bank[:].bitcast(BF16), AF.Identity))
                reads(bank, sS, prom)
                kt_done[sub] = prom
            sbk.flush()
            v_ktm = sS.n
            if tile == 0:
                dump(k_tm[:], "k_tm", [128, NSUB * D], BF16, [(sS, v_ktm)])

            z0_done = {}
            for hp in range(NHP):
                hA, hB = 2 * hp, 2 * hp + 1

                @blk.tensor
                def _(t, hp=hp):
                    twait(t, ps_o)
                    W(t, sG, q_done[hp])
                    if tau > 0:
                        W(t, sV, st_ev)
                        sT.inc(t.matmul(ps_o[0:64, :], st_bf[0:64, 64 * hp:64 * hp + 64],
                                        q_fm[0:64, T * hp:T * (hp + 1)],
                                        start=True, stop=False, tile_position=(0, 0)))
                        sT.inc(t.matmul(ps_o[64:128, :], st_bf[64:128, 64 * hp:64 * hp + 64],
                                        q_fm[64:128, T * hp:T * (hp + 1)],
                                        start=True, stop=False, tile_position=(64, 64)))
                        bread(st_bf, sT, sT.n)

                tri_v = {}

                def emit_score(ksub):
                    q0 = 128 * ksub
                    sbp = s_bf[ksub % 2]
                    bA, bB = (ps_sA, ps_sB) if ksub % 2 == 0 else (ps_p0, ps_p1)

                    @blk.tensor
                    def _(t, hp=hp, ksub=ksub, q0=q0, bA=bA, bB=bB):
                        W(t, sG, q_done[hp])
                        W(t, sG, k_done[hp])
                        twait(t, bA)
                        twait(t, bB)
                        sT.inc(t.matmul(bA[:, q0:T],
                                        k_fm[0:64, T * hp + q0:T * hp + q0 + 128],
                                        q_fm[0:64, T * hp + q0:T * (hp + 1)],
                                        start=True, stop=True, tile_position=(0, 0)))
                        sT.inc(t.matmul(bB[:, q0:T],
                                        k_fm[64:128, T * hp + q0:T * hp + q0 + 128],
                                        q_fm[64:128, T * hp + q0:T * (hp + 1)],
                                        start=True, stop=True, tile_position=(64, 0)))
                    t_s = sT.n

                    @blk.vector
                    def _(v, ksub=ksub, q0=q0, sbp=sbp, bA=bA, bB=bB, t_s=t_s):
                        W(v, sT, t_s)
                        bwait(v, sbp[0])
                        bwait(v, sbp[1])
                        sV.inc(v.tensor_tensor(sbp[0][:, q0:q0 + 128], bA[:, q0:q0 + 128],
                                               tri_bf[:], ALU.mult))
                        sV.inc(v.tensor_tensor(sbp[1][:, q0:q0 + 128], bB[:, q0:q0 + 128],
                                               tri_bf[:], ALU.mult))
                        if ksub < NSUB - 1:
                            sV.inc(v.tensor_copy(sbp[0][:, q0 + 128:T], bA[:, q0 + 128:T]))
                            sV.inc(v.tensor_copy(sbp[1][:, q0 + 128:T], bB[:, q0 + 128:T]))
                    tri_v[ksub] = sV.n
                    reads(bA, sV, tri_v[ksub])
                    reads(bB, sV, tri_v[ksub])

                def emit_ev(ksub):
                    q0 = 128 * ksub
                    sbp = s_bf[ksub % 2]

                    @blk.tensor
                    def _(t, hp=hp, ksub=ksub, q0=q0, hA=hA, hB=hB, sbp=sbp):
                        W(t, sV, tri_v[ksub])
                        W(t, sS, vv_done[(ksub, hp // 4)])
                        sT.inc(t.matmul(ps_o[0:64, q0:T],
                                        v_tm[:, D * ksub + 64 * hA:D * ksub + 64 * hA + 64],
                                        sbp[0][:, q0:T],
                                        start=(tau == 0 and ksub == 0), stop=False,
                                        tile_position=(0, 0)))
                        sT.inc(t.matmul(ps_o[64:128, q0:T],
                                        v_tm[:, D * ksub + 64 * hB:D * ksub + 64 * hB + 64],
                                        sbp[1][:, q0:T],
                                        start=(tau == 0 and ksub == 0), stop=False,
                                        tile_position=(0, 64)))
                        bread(sbp[0], sT, sT.n)
                        bread(sbp[1], sT, sT.n)
                        W(t, sS, kt_done[ksub])
                        if st_ev is not None:
                            W(t, sV, st_ev)
                        sT.inc(t.matmul(ps_state[0:64, 64 * hp:64 * hp + 64],
                                        k_tm[:, D * ksub + 128 * hp:D * ksub + 128 * hp + 64],
                                        v_tm[:, D * ksub + 64 * hA:D * ksub + 64 * hA + 64],
                                        start=(tau == 0 and hp == 0 and ksub == 0), stop=False,
                                        tile_position=(0, 0)))
                        sT.inc(t.matmul(ps_state[64:128, 64 * hp:64 * hp + 64],
                                        k_tm[:, D * ksub + 128 * hp + 64:D * ksub + 128 * hp + 128],
                                        v_tm[:, D * ksub + 64 * hB:D * ksub + 64 * hB + 64],
                                        start=(tau == 0 and hp == 0 and ksub == 0), stop=False,
                                        tile_position=(0, 64)))

                emit_score(0)
                emit_score(1)
                emit_ev(0)
                emit_score(2)
                emit_ev(1)
                emit_score(3)
                emit_ev(2)
                emit_ev(3)
                t_pair = sT.n

                @blk.vector
                def _(v, hp=hp):
                    W(v, sT, t_pair)
                    W(v, sG, v_gate)
                    bwait(v, sq_bf[hp % 2])
                    VS(v, v.tensor_tensor(z0[:, T * hp:T * (hp + 1)], ps_o[:],
                                          gate_bf[:, T * hp:T * (hp + 1)], ALU.mult))
                    sV.inc(v.tensor_tensor(sq_bf[hp % 2][:], z0[:, T * hp:T * (hp + 1)],
                                           z0[:, T * hp:T * (hp + 1)], ALU.mult))
                reads(ps_o, sV, sV.n)
                bread(gate_bf, sV, sV.n)
                z0_done[hp] = sV.n

                @blk.tensor
                def _(t, cc=hp):
                    if cc == 0:
                        twait(t, ps_m0)
                        twait(t, ps_m1)
                    W(t, sV, z0_done[cc])
                    sT.inc(t.matmul(ps_m0[0:1, :], ones128[:], z0[:, T * cc:T * (cc + 1)],
                                    start=(cc == 0), stop=(cc == NCH - 1)))
                    sT.inc(t.matmul(ps_m1[0:1, :], ones128[:], sq_bf[cc % 2][:],
                                    start=(cc == 0), stop=(cc == NCH - 1)))
                bread(sq_bf[hp % 2], sT, sT.n)
                emit_prep_spans(2)
            v_z0 = sV.n
            t_st = sT.n
            if tile == 0:
                dump(z0[:], "z0", [128, NCH * T], BF16, [(sV, v_z0)])

            @blk.vector
            def _(v):
                W(v, sT, sT.n)
                bwait(v, st_bf)
                sV.inc(v.tensor_copy(st_bf[:], ps_state[:]))
            st_ev = sV.n
            emit_prep_spans(4)
            if tile == 0:
                dump(st_bf[:], "st_bf", [128, T], BF16, [(sV, st_ev)])


            @blk.vector
            def _(v):
                W(v, sT, t_st)
                bwait(v, hlrow)
                VS(v, v.tensor_copy(hlrow[0:1, 0:T], ps_m0[0:1, :]))
                VS(v, v.tensor_copy(hlrow[0:1, T:2 * T], ps_m1[0:1, :]))
                VS(v, v.tensor_tensor(hlrow[0:1, 2 * T:3 * T], ps_m0[0:1, :],
                                      hlrow[0:1, 0:T], ALU.subtract))
                VS(v, v.tensor_tensor(hlrow[0:1, 3 * T:4 * T], ps_m1[0:1, :],
                                      hlrow[0:1, T:2 * T], ALU.subtract))
                sV.inc(v.tensor_scalar(murow[0:1, :], ps_m0[0:1, :], 1.0 / D, None, ALU.mult))
            v_str = sV.n
            reads(ps_m0, sV, v_str)
            reads(ps_m1, sV, v_str)
            v_hl = v_str

            @blk.tensor
            def _(t):
                W(t, sV, v_hl)
                twait(t, ps_m0)
                for j in range(NSUB):
                    for stat in range(2):
                        col = 4 * stat + j
                        off = T * stat + 128 * j
                        sT.inc(t.matmul(ps_m0[:, col:col + 1], hlrow[0:1, off:off + 128],
                                        onesrow[0:1, 0:1], start=True, stop=False))
                        sT.inc(t.matmul(ps_m0[:, col:col + 1], hlrow[0:1, 2 * T + off:2 * T + off + 128],
                                        onesrow[0:1, 0:1], start=False, stop=True))
                bread(hlrow, sT, sT.n)
            t_tr = sT.n

            @blk.vector
            def _(v):
                W(v, sT, t_tr)
                VS(v, v.tensor_copy(ln_sc[:, 24:32], ps_m0[:, 0:8]))
                VS(v, v.tensor_scalar(ln_sc[:, 32:36], ln_sc[:, 24:28], 1.0 / D, None, ALU.mult))
                VS(v, v.tensor_tensor(ln_sc[:, 36:40], ln_sc[:, 32:36], ln_sc[:, 32:36], ALU.mult))
                VS(v, v.scalar_tensor_tensor(ln_sc[:, 40:44], ln_sc[:, 28:32], 1.0 / D,
                                             ln_sc[:, 36:40], ALU.mult, ALU.subtract))
                VS(v, v.tensor_scalar(ln_sc[:, 40:44], ln_sc[:, 40:44], 1e-5, None, ALU.add))
                VS(v, v.tensor_scalar(ln_si[:, 4:8], ln_sc[:, 40:44].bitcast(I32), 1, None,
                                      ALU.logical_shift_right))
                VS(v, v.tensor_scalar(ln_si[:, 4:8], ln_si[:, 4:8], -1, 0x5f3759df, ALU.mult, ALU.add))
                VS(v, v.tensor_copy(ln_sc[:, 44:48], ln_si[:, 4:8].bitcast(F32)))
                for _ in range(3):
                    VS(v, v.tensor_tensor(ln_sc[:, 48:52], ln_sc[:, 44:48], ln_sc[:, 44:48], ALU.mult))
                    VS(v, v.tensor_tensor(ln_sc[:, 48:52], ln_sc[:, 48:52], ln_sc[:, 40:44], ALU.mult))
                    VS(v, v.tensor_scalar(ln_sc[:, 48:52], ln_sc[:, 48:52], -0.5, 1.5, ALU.mult, ALU.add))
                    VS(v, v.tensor_tensor(ln_sc[:, 44:48], ln_sc[:, 44:48], ln_sc[:, 48:52], ALU.mult))
            v_ln2 = sV.n
            reads(ps_m0, sV, v_ln2)
            if tile == 0:
                dump(ln_sc[:], "ln_sc", [128, 64], F32, [(sV, v_ln2)])

            nxt = tile + 1
            s2, tau2 = divmod(nxt, NT)
            for sub in range(NSUB):
                @blk.vector
                def _(v, sub=sub):
                    W(v, sV, v_woprep)
                    VS(v, v.scalar_tensor_tensor(xld[sub][:], xld[sub][:],
                                                 ln_sc[:, 52 + sub:53 + sub],
                                                 co_bc[:], ALU.mult, ALU.add))
                v_u = sV.n

                for o in range(2):
                    bank = ps_sA if o == 0 else ps_sB

                    @blk.tensor
                    def _(t, sub=sub, o=o, bank=bank):
                        W(t, sV, v_z0)
                        if tile == 0:
                            W(t, sS, v_wo_ready)
                            W(t, sV, v_woprep)
                        twait(t, bank)
                        for cc in range(NCH):
                            sT.inc(t.matmul(bank[:],
                                            z0[:, T * cc + 128 * sub:T * cc + 128 * sub + 128],
                                            wo_sb[:, D * cc + 512 * o:D * cc + 512 * o + 512],
                                            start=(cc == 0), stop=False))

                    @blk.tensor
                    def _(t, sub=sub, o=o, bank=bank):
                        W(t, sV, v_str)
                        sT.inc(t.matmul(bank[:], murow[0:1, 128 * sub:128 * (sub + 1)],
                                        nB_bf[0:1, 512 * o:512 * (o + 1)],
                                        start=False, stop=True))
                    t_wo = sT.n

                    @blk.vector
                    def _(v, sub=sub, o=o, bank=bank):
                        W(v, sT, t_wo)
                        W(v, sV, v_ln2)
                        W(v, sV, v_u)
                        if o == 0:
                            bwait(v, x1buf[sub % 2])
                        sV.inc(v.scalar_tensor_tensor(
                            x1buf[sub % 2][:, 512 * o:512 * (o + 1)], bank[:],
                            ln_sc[:, 44 + sub:45 + sub],
                            xld[sub][:, 512 * o:512 * (o + 1)], ALU.mult, ALU.add))
                    reads(bank, sV, sV.n)
                    if o == 1:
                        bread(xld[sub], sV, sV.n)
                v_x1 = sV.n

                @blk.sync
                def _(y, sub=sub):
                    W(y, sV, v_x1)
                    sY.incd(y.dma_start(out=scr_x1[s, t0 + 128 * sub:t0 + 128 * (sub + 1), :],
                                        in_=x1buf[sub % 2][:]))
                bread(x1buf[sub % 2], sY, sY.n)

                if nxt < SPC * NT:
                    emit_x_stats(s2, T * tau2, sub)
            if nxt < SPC * NT:
                emit_ln1_newton()

            emit_prep_chunks(4)

    emit_prep_slice(10 ** 9)
    v_x1_stores = sY.n
    v_prep_q = sQ.n

    stA.close()
    EB = st.enter_context
    w1_sb = EB(nc.sbuf_tensor("w1_sb", [128, NCH * DH], BF16))
    w2_sb = EB(nc.sbuf_tensor("w2_sb", [128, 32 * D], BF16))
    g_fm = EB(nc.sbuf_tensor("g_fm", [128, 32 * T], BF16))
    xn3_tm = [EB(nc.sbuf_tensor(f"xn3_tm{i}", [128, D], BF16)) for i in range(2)]
    xn3_fm = EB(nc.sbuf_tensor("xn3_fm", [128, NCH * T], BF16))
    xbld = [EB(nc.sbuf_tensor(f"xbld{i}", [128, D], F32)) for i in range(4)]
    ybuf = [EB(nc.sbuf_tensor(f"ybuf{i}", [128, D], F32)) for i in range(2)]

    vT_A, vV_A, vS_A, vG_A = sT.n, sV.n, sS.n, sG.n

    @blk.sync
    def _(y):
        W(y, sT, vT_A)
        W(y, sV, vV_A)
        W(y, sS, vS_A)
        W(y, sG, vG_A)
        W(y, sQ, v_prep_q)
        W(y, sY, v_x1_stores)
        for cc in range(NCH):
            sY.incd(y.dma_start_transpose(w1_sb[:, DH * cc:DH * (cc + 1)],
                                          scr_w1[:, 128 * cc:128 * (cc + 1)]))
    v_w1 = sY.n

    @blk.gpsimd
    def _(g):
        W(g, sT, vT_A)
        W(g, sV, vV_A)
        W(g, sS, vS_A)
        W(g, sQ, v_prep_q)
        sQ.incd(g.dma_start(out=w2_sb[:].rearrange("p (c j) -> p c j", c=32), in_=scr_w2.rearrange("(c p) j -> p c j", p=128)))
    v_w2 = sQ.n

    @blk.vector
    def _(v):
        W(v, sV, vV_A)
        sV.inc(v.tensor_tensor(c1_sb[:], c1_sb[:], vecs[:, 18:50], ALU.add))
    v_c1 = sV.n

    def emit_xb_stats(s2, t02, sub):
        @blk.sync
        def _(y, sub=sub):
            bwait(y, xbld[sub])
            sY.incd(y.dma_start(out=xbld[sub][:],
                                in_=scr_x1[s2, t02 + 128 * sub:t02 + 128 * (sub + 1), :]))
        xv = sY.n

        @blk.vector
        def _(v, sub=sub, xv=xv):
            W(v, sY, xv)
            sV.inc(v.bn_stats(bn6[:, 0:6], xbld[sub][:, 0:512]))
            VS(v, v.bn_stats(bn6[:, 6:12], xbld[sub][:, 512:1024]))
            VS(v, v.bn_aggr(ln_sc[:, 2 * sub:2 * sub + 2], bn6[:]))

    for sub in range(NSUB):
        emit_xb_stats(0, 0, sub)
    emit_ln1_newton()

    for g8 in range(SPC * NT):
        s, tau = divmod(g8, NT)
        t0 = T * tau
        nxt = g8 + 1
        s2, tau2 = divmod(nxt, NT)

        snv = []
        sbe3 = SBatch()
        for sub in range(NSUB):
            xnb = xn3_tm[sub % 2]

            @blk.vector
            def _(v, sub=sub, xnb=xnb):
                bwait(v, xnb)
                VS(v, v.tensor_scalar(xbld[sub][:], xbld[sub][:],
                                      ln_sc[:, 12 + sub:13 + sub], None, ALU.mult))
                VS(v, v.tensor_scalar(xnb[:], xbld[sub][:],
                                      ln_sc[:, 20 + sub:21 + sub], None, ALU.add))
            snv.append(sV.n)

            bank = ps_m0 if sub % 2 == 0 else ps_m1

            @blk.tensor
            def _(t, sub=sub, bank=bank, xnb=xnb):
                W(t, sV, snv[sub])
                twait(t, bank)
                pb = bank[:].bitcast(BF16)
                for cc in range(NCH):
                    sT.inc(t.transpose(pb[:, 128 * cc:128 * (cc + 1)],
                                       xnb[:, 128 * cc:128 * (cc + 1)], ident[:]))
            t_tp = sT.n
            bread(xnb, sT, sT.n)

            def _evac3(sc, sub=sub, bank=bank):
                out = xn3_fm[:].rearrange("p (c t) -> p c t", c=NCH)[:, :, 128 * sub:128 * (sub + 1)]
                return sc.activation(out, bank[:].bitcast(BF16).rearrange("p (c t) -> p c t", c=NCH), AF.Identity)
            prom = sbe3.add([(sT, t_tp)], _evac3)
            reads(bank, sS, prom)
        sbe3.flush()
        v_xn3 = sS.n

        sbh = SBatch()
        hbanks = [ps_p0, ps_p1, ps_sA]
        for m in range(32):
            bank = hbanks[m % 3]

            @blk.tensor
            def _(t, m=m, bank=bank):
                W(t, sS, v_xn3)
                if g8 == 0:
                    W(t, sY, v_w1)
                    W(t, sV, v_c1)
                twait(t, bank)
                for cc in range(NCH):
                    sT.inc(t.matmul(bank[:], w1_sb[:, DH * cc + 128 * m:DH * cc + 128 * m + 128],
                                    xn3_fm[:, T * cc:T * (cc + 1)],
                                    start=(cc == 0), stop=(cc == NCH - 1)))
            t_h = sT.n

            prom = sbh.add([(sT, t_h)],
                           lambda sc, m=m, bank=bank: sc.activation(
                               g_fm[:, T * m:T * (m + 1)], bank[:], AF.Gelu,
                               bias=c1_sb[:, m:m + 1]))
            reads(bank, sS, prom)
        sbh.flush()
        v_g = sS.n

        for sub in range(NSUB):
            for o in range(2):
                bank = ps_sB if o == 0 else ps_o

                @blk.tensor
                def _(t, sub=sub, o=o, bank=bank):
                    W(t, sS, v_g)
                    if g8 == 0:
                        W(t, sQ, v_w2)
                    twait(t, bank)
                    for cc in range(32):
                        sT.inc(t.matmul(bank[:],
                                        g_fm[:, T * cc + 128 * sub:T * cc + 128 * sub + 128],
                                        w2_sb[:, D * cc + 512 * o:D * cc + 512 * o + 512],
                                        start=(cc == 0), stop=False))
                    sT.inc(t.matmul(bank[:], onesrow[:], c2_bf[0:1, 512 * o:512 * (o + 1)],
                                    start=False, stop=True))
                t_y = sT.n

                @blk.vector
                def _(v, sub=sub, o=o, bank=bank, t_y=t_y):
                    W(v, sT, t_y)
                    if o == 0:
                        bwait(v, ybuf[sub % 2])
                    sV.inc(v.scalar_tensor_tensor(
                        ybuf[sub % 2][:, 512 * o:512 * (o + 1)],
                        xbld[sub][:, 512 * o:512 * (o + 1)],
                        ln_sc[:, 52 + sub:53 + sub],
                        bank[:], ALU.mult, ALU.add))
                reads(bank, sV, sV.n)
                if o == 1:
                    bread(xbld[sub], sV, sV.n)
            v_y = sV.n

            @blk.sync
            def _(y, sub=sub):
                W(y, sV, v_y)
                sY.incd(y.dma_start(out=ap_y[s, t0 + 128 * sub:t0 + 128 * (sub + 1), :],
                                    in_=ybuf[sub % 2][:]))
            bread(ybuf[sub % 2], sY, sY.n)

            if nxt < SPC * NT:
                emit_xb_stats(s2, T * tau2, sub)
        if nxt < SPC * NT:
            emit_ln1_newton()

    if _DEBUG:
        @blk.sync
        def _(y):
            W(y, sQ, sQ.n)
            W(y, sV, sV.n)
            W(y, sT, sT.n)
            W(y, sS, sS.n)
            sY.incd(y.dma_start(out=dbg_out("scr_x1", [SPC, N, D], F32), in_=scr_x1))
            sY.incd(y.dma_start(out=dbg_out("scr_w1", [DH, D], BF16), in_=scr_w1))
            sY.incd(y.dma_start(out=dbg_out("scr_w2", [DH, D], BF16), in_=scr_w2))
            sY.incd(y.dma_start(out=dbg_out("c1_sb", [128, 32], F32), in_=c1_sb[:]))

    v_final = sY.n

    @blk.gpsimd
    def _(g):
        W(g, sY, v_final)

    st.close()
    return nc


_NC_CACHE = None


def kernel(**inputs):
    global _NC_CACHE
    x = np.ascontiguousarray(np.asarray(inputs["L0_attn_x"], np.float32))
    staged = _host_stage(inputs)
    if _NC_CACHE is None:
        _NC_CACHE = build_nc()
    nc = _NC_CACHE
    in_maps = []
    for c in range(NCORES):
        m = dict(staged)
        m["x"] = np.ascontiguousarray(x[SPC * c:SPC * (c + 1)])
        in_maps.append(m)
    res = run_bass_kernel_spmd(nc, in_maps, core_ids=list(range(NCORES)))
    out = np.concatenate([np.asarray(r["y"], np.float32) for r in res.results], axis=0)
    return out
